# version 29
# baseline (speedup 1.0000x reference)
import math
import numpy as np
import ml_dtypes
from contextlib import ExitStack

import concourse.bass as bass
import concourse.mybir as mybir
from concourse.bass_utils import run_bass_kernel_spmd
from concourse.alu_op_type import AluOpType as OP

F32 = mybir.dt.float32
BF16 = mybir.dt.bfloat16
BF = ml_dtypes.bfloat16
AF = mybir.ActivationFunctionType
AX = mybir.AxisListType

B, TFULL, D = 2, 8192, 1024
NCORE = 8
T = 2048
NT = T // 128
H, M, DK = 16, 256, 64
I = 2816
IG = I // 8
DG = D // 8
EPS = 1e-6
E16 = EPS / 16.0
LN16 = math.log(16.0)
QCH = 512
DEBUG_DUMPS = False


class Sched:
    ENGS = ("sync", "gpsimd", "vector", "scalar", "tensor")

    def __init__(self):
        self.ops = []
        self.mark = 0
        self.count = {e: 0 for e in self.ENGS}
        for e in self.ENGS:
            self.count[e + "_dma"] = 0
            self.count[e + "_echo"] = 0
        self.count["cc"] = 0
        self.count["x2t_dma"] = 0
        self.count["hqt_dma"] = 0
        self.count["w_dma"] = 0
        self.count["wqin_dma"] = 0
        self.last_w = {}
        self.readers = {}
        self.echo = {}

    QSRC = {"sync_dma": "sync", "gpsimd_dma": "gpsimd", "scalar_dma": "scalar",
            "vector_dma": "vector", "tensor_dma": "tensor", "cc": "gpsimd",
            "x2t_dma": "sync", "hqt_dma": "sync", "w_dma": "sync",
            "wqin_dma": "gpsimd"}

    def _echo(self, ws):
        cum = self.count[ws]
        last_cum, tok = self.echo.get(ws, (-1, None))
        if last_cum >= cum:
            return tok
        qeng = self.QSRC[ws]
        es = qeng + "_echo"
        self.count[es] += 1
        tok = (es, self.count[es])
        self.ops.append((qeng, lambda en: en.nop(), ((ws, cum),), es, 1))
        self.echo[ws] = (cum, tok)
        return tok

    def emit(self, eng, fn, reads=(), writes=(), dma=False, cc=False, extra_waits=()):
        waits = set()
        for b in reads:
            t = self.last_w.get(b)
            if t is not None:
                waits.add(t)
        for b in writes:
            t = self.last_w.get(b)
            if t is not None:
                waits.add(t)
            for t2 in self.readers.get(b, ()):
                waits.add(t2)
        for t in extra_waits:
            if t is not None:
                waits.add(t)
        stream = dma if isinstance(dma, str) else None
        fixed = set()
        for (ws, wv) in waits:
            if (ws.endswith("_dma") or ws == "cc") and self.QSRC.get(ws) != eng:
                fixed.add(self._echo(ws))
            else:
                fixed.add((ws, wv if not (ws.endswith("_dma") or ws == "cc")
                           else self.count[ws]))
        waits = fixed
        sem = "cc" if cc else (stream if stream else (eng + "_dma" if dma else eng))
        amt = 1 if cc else (16 if dma else 1)
        self.count[sem] += amt
        tok = (sem, self.count[sem])
        self.ops.append((eng, fn, tuple(waits), sem, amt))
        self.echo.pop(sem, None)
        for b in writes:
            self.last_w[b] = tok
            self.readers[b] = []
        for b in reads:
            self.readers.setdefault(b, []).append(tok)
        return tok

    def barrier(self):
        toks = []
        for e in self.ENGS:
            own = [(e, self.count[e]), (e + "_dma", self.count[e + "_dma"])]
            if e == "sync":
                own += [(s, self.count[s]) for s in ("x2t_dma", "hqt_dma", "w_dma")]
            if e == "gpsimd":
                own += [("wqin_dma", self.count["wqin_dma"])]
            toks.append(self.emit(e, lambda en: en.nop(), extra_waits=own))
        for e in self.ENGS:
            self.emit(e, lambda en: en.nop(), extra_waits=toks)

    def replay_new(self, block, sems):
        lo, hi = self.mark, len(self.ops)
        self.mark = hi
        sched = self

        def mk(eng_name):
            def go(eng):
                watermark = {}
                for idx in range(lo, hi):
                    e, fn, waits, sem, amt = sched.ops[idx]
                    if e != eng_name:
                        continue
                    for (ws, wv) in sorted(waits):
                        if watermark.get(ws, 0) >= wv:
                            continue
                        eng.wait_ge(sems[ws], wv)
                        watermark[ws] = wv
                    fn(eng).then_inc(sems[sem], amt)
            return go

        block.sync(mk("sync"))
        block.gpsimd(mk("gpsimd"))
        block.vector(mk("vector"))
        block.scalar(mk("scalar"))
        block.tensor(mk("tensor"))


def build_nc():
    nc = bass.Bass("TRN2", target_bir_lowering=False)

    dx = nc.dram_tensor("x", [T, D], F32, kind="ExternalInput").ap()
    dwq = nc.dram_tensor("wqt", [D, D], F32, kind="ExternalInput").ap()
    dwk = nc.dram_tensor("wkt", [D, D], F32, kind="ExternalInput").ap()
    dwv = nc.dram_tensor("wvt", [D, D], F32, kind="ExternalInput").ap()
    dwo = nc.dram_tensor("wot", [D, D], F32, kind="ExternalInput").ap()
    dbo = nc.dram_tensor("bo", [1, D], F32, kind="ExternalInput").ap()
    dpj = nc.dram_tensor("projp", [8, 128, M], F32, kind="ExternalInput").ap()
    dwg = nc.dram_tensor("wg_s", [IG, D], F32, kind="ExternalInput").ap()
    dwu = nc.dram_tensor("wu_s", [IG, D], F32, kind="ExternalInput").ap()
    dwd = nc.dram_tensor("wd_s", [DG, I], F32, kind="ExternalInput").ap()
    dout = nc.dram_tensor("out", [T, D], F32, kind="ExternalOutput").ap()

    NW_SH = IG * D * 2 + DG * I
    cc_w_in = nc.dram_tensor("ccw_in", [1, NW_SH], BF16)
    cc_w_out = nc.dram_tensor("ccw_out", [8, NW_SH], BF16)
    cc_kv_in = nc.dram_tensor("cckv_in", [H, 65, M], F32)
    cc_kv_out = nc.dram_tensor("cckv_out", [H, 65, M], F32)
    d_x1 = nc.dram_tensor("x1scr", [T, D], F32)

    dbg = {}
    def dump(name, ap_fn, shape, buf, dt=F32):
        if not DEBUG_DUMPS:
            return
        dbg[name] = nc.dram_tensor("dbg_" + name, shape, dt, kind="ExternalOutput")
        E("sync", lambda g, nm=name: g.dma_start(out=dbg[nm].ap(), in_=ap_fn()),
          reads=[buf], dma=True)

    es0 = ExitStack()
    sems = {}
    for s in Sched.ENGS + ("cc",):
        sems[s] = es0.enter_context(nc.semaphore("sem_" + s))
    for s in ("sync", "gpsimd", "scalar", "vector", "tensor"):
        sems[s + "_dma"] = es0.enter_context(nc.semaphore("sem_" + s + "d"))
        sems[s + "_echo"] = es0.enter_context(nc.semaphore("sem_" + s + "e"))
    for s in ("x2t_dma", "hqt_dma", "w_dma", "wqin_dma"):
        sems[s] = es0.enter_context(nc.semaphore("sem_" + s))
    S = Sched()
    E = S.emit

    def sb(stk, name, shape, dt=F32, side=None):
        return stk.enter_context(nc.sbuf_tensor(name, shape, dt, side=side))

    def psm(stk, name, shape, dt=F32):
        return stk.enter_context(nc.psum_tensor(name, shape, dt))

    pb = [psm(es0, "pb%d" % i, [128, 512], F32) for i in range(6)]
    pbf = psm(es0, "pbf", [128, 1024], BF16)
    pb7 = psm(es0, "pb7", [128, 512], F32)

    consts = sb(es0, "consts", [128, 4], F32)
    onescol = sb(es0, "onescol", [128, 1], BF16)
    ones_row = sb(es0, "ones_row", [1, 256], BF16)
    id128 = sb(es0, "id128", [128, 128], BF16)
    id128f = sb(es0, "id128f", [128, 128], F32)
    bo_row = sb(es0, "bo_row", [1, D], BF16)
    projp = sb(es0, "projp_s", [128, 8, M], BF16)
    onesblk = sb(es0, "onesblk", [128, 2], BF16)
    xs_ = [sb(es0, "xs%d" % i, [128, D], F32) for i in range(2)]
    sm = sb(es0, "sm", [128, 24], F32)

    def mkquant(stk, side=None, tag="", nsets=2):
        q = {"sets": [], "i": [0], "tag": tag}
        for si in range(nsets):
            s = {}
            for nm, shape, dt_ in (("m", [128, QCH // 16], F32),
                                   ("rc", [128, QCH // 16], F32),
                                   ("r", [128, QCH // 16], F32),
                                   ("s2", [128, QCH // 16], F32),
                                   ("ab", [128, QCH], F32),
                                   ("an", [128, QCH], F32),
                                   ("b", [128, QCH], F32),
                                   ("a", [128, QCH], F32),
                                   ("sg", [128, QCH], F32)):
                key = "q_%s%s%d" % (nm, tag, si)
                s[nm] = (sb(stk, key, shape, dt_, side=side), key)
            q["sets"].append(s)
        return q

    def emit_quant_chunk(q, xin, xout, Fc, in_bufs, out_buf, P=128):
        NB = Fc // 16
        s = q["sets"][q["i"][0] % len(q["sets"])]
        q["i"][0] += 1
        m, mN = s["m"]; rc, rcN = s["rc"]; r, rN = s["r"]; s2, s2N = s["s2"]
        ab, abN = s["ab"]; an, anN = s["an"]; b, bN = s["b"]; a, aN = s["a"]
        sg, sgN = s["sg"]
        E("scalar", lambda ss: ss.activation(ab[0:P, 0:Fc], xin, AF.Abs, scale=6.0),
          reads=list(in_bufs), writes=[abN])
        E("scalar", lambda ss: ss.activation(sg[0:P, 0:Fc], xin, AF.Sign),
          reads=list(in_bufs), writes=[sgN])
        E("vector", lambda v: v.tensor_reduce(
            m[0:P, 0:NB], xin.rearrange("p (b k) -> p b k", k=16), AX.X, OP.max,
            apply_absolute_value=True), reads=list(in_bufs), writes=[mN])
        E("vector", lambda v: v.tensor_single_scalar(m[0:P, 0:NB], m[0:P, 0:NB], 6e-12, OP.max),
          writes=[mN])
        E("vector", lambda v: v.reciprocal(r[0:P, 0:NB], m[0:P, 0:NB]),
          reads=[mN], writes=[rN])
        E("vector", lambda v: v.tensor_single_scalar(s2[0:P, 0:NB], m[0:P, 0:NB], 1.0 / 12.0, OP.mult),
          reads=[mN], writes=[s2N])
        E("vector", lambda v: v.tensor_tensor(
            an[0:P, 0:Fc].rearrange("p (b k) -> p b k", k=16),
            ab[0:P, 0:Fc].rearrange("p (b k) -> p b k", k=16),
            r[0:P, 0:NB].unsqueeze(2).broadcast_to([P, NB, 16]), OP.mult),
          reads=[abN, rN], writes=[anN])
        E("gpsimd", lambda g: g.tensor_scalar(
            b[0:P, 0:Fc], an[0:P, 0:Fc], 5.0, 2.0, OP.is_gt, OP.mult),
          reads=[anN], writes=[bN])
        for thr in (3.5, 2.5):
            E("vector", lambda v, th=thr: v.scalar_tensor_tensor(
                b[0:P, 0:Fc], an[0:P, 0:Fc], th, b[0:P, 0:Fc], OP.is_gt, OP.add),
              reads=[anN], writes=[bN])
        E("vector", lambda v: v.tensor_single_scalar(
            a[0:P, 0:Fc], an[0:P, 0:Fc], 0.25, OP.is_gt),
          reads=[anN], writes=[aN])
        for thr in (0.75, 1.25, 1.75):
            E("vector", lambda v, th=thr: v.scalar_tensor_tensor(
                a[0:P, 0:Fc], an[0:P, 0:Fc], th, a[0:P, 0:Fc], OP.is_gt, OP.add),
              reads=[anN], writes=[aN])
        E("vector", lambda v: v.scalar_tensor_tensor(
            a[0:P, 0:Fc], b[0:P, 0:Fc], 2.0, a[0:P, 0:Fc], OP.mult, OP.add),
          reads=[bN], writes=[aN])
        E("vector", lambda v: v.tensor_tensor(
            a[0:P, 0:Fc].rearrange("p (b k) -> p b k", k=16),
            a[0:P, 0:Fc].rearrange("p (b k) -> p b k", k=16),
            s2[0:P, 0:NB].unsqueeze(2).broadcast_to([P, NB, 16]), OP.mult),
          reads=[s2N], writes=[aN])
        return E("vector", lambda v: v.tensor_tensor(
            xout, a[0:P, 0:Fc], sg[0:P, 0:Fc], OP.mult),
            reads=[aN, sgN], writes=[out_buf])

    def emit_quant(q, xin_fn, xout_fn, Fcols, in_bufs, out_buf):
        c0 = 0
        while c0 < Fcols:
            fc = min(QCH, Fcols - c0)
            emit_quant_chunk(q, xin_fn(c0, fc), xout_fn(c0, fc), fc, in_bufs, out_buf)
            c0 += fc

    def emit_ln(sqbuf, xbuf, xname, outap, outbuf):
        E("vector", lambda v: v.tensor_reduce(sm[:, 0:1], xbuf[:, :], AX.X, OP.add),
          reads=[xname], writes=["sm"])
        E("vector", lambda v: v.tensor_single_scalar(sm[:, 0:1], sm[:, 0:1], 1.0 / D, OP.mult),
          writes=["sm"])
        E("scalar", lambda s: s.activation(sqbuf[:, 0:D], xbuf[:, :], AF.Square,
                                           accum_out=sm[:, 4:5]),
          reads=[xname], writes=["sq_scr", "sm"])
        E("vector", lambda v: v.tensor_single_scalar(sm[:, 4:5], sm[:, 4:5], 1.0 / D, OP.mult),
          writes=["sm"])
        E("vector", lambda v: v.scalar_tensor_tensor(
            sm[:, 5:6], sm[:, 0:1], sm[:, 0:1], sm[:, 4:5], OP.mult, OP.subtract),
          writes=["sm"])
        E("scalar", lambda s: s.activation(sm[:, 6:7], sm[:, 5:6], AF.Sqrt,
                                           bias=consts[:, 0:1], scale=-1.0),
          reads=["consts"], writes=["sm"])
        E("vector", lambda v: v.reciprocal(sm[:, 7:8], sm[:, 6:7]), writes=["sm"])
        E("vector", lambda v: v.tensor_scalar(
            sm[:, 8:9], sm[:, 0:1], sm[:, 7:8], -1.0, OP.mult, OP.mult), writes=["sm"])
        return E("scalar", lambda s: s.activation(
            outap, xbuf[:, :], AF.Identity, bias=sm[:, 8:9], scale=sm[:, 7:8]),
            reads=[xname, "sm"], writes=[outbuf])

    block_cm = nc.Block()
    block = block_cm.__enter__()

    es1 = ExitStack()
    iota_s = sb(es1, "iota_s", [128, 128], F32, side="right")
    X1T = sb(es1, "X1T", [128, 8, T], BF16, side="right")
    w_aL = sb(es1, "w_aL", [128, 8, 512], BF16, side="right")
    w_aH = sb(es1, "w_aH", [128, 8, 512], BF16, side="right")
    w_kh = sb(es1, "w_kh", [128, 8, 512], BF16, side="right")
    xn_s = sb(es1, "xn_s", [128, 2, D], BF16, side="right")
    wq_in_ = [sb(es1, "wq_in%d" % i, [128, QCH], F32, side="right") for i in range(2)]
    wq_out_ = [sb(es1, "wq_out%d" % i, [128, QCH], BF16, side="right") for i in range(2)]
    wtr_ = [sb(es1, "wtr%d" % i, [128, QCH], BF16, side="right") for i in range(2)]
    sq1 = sb(es1, "sq1", [128, 2048], BF16, side="right")
    q1 = mkquant(es1, side="right")
    kfw = [sb(es1, "kfw%d" % i, [128, 2, T], BF16, side="right") for i in range(2)]
    kvst = sb(es1, "kvst", [65, 4, M], F32, side="right")

    es2 = ExitStack()
    Qf = sb(es2, "Qf", [128, 8, T], BF16)
    Vt = sb(es2, "Vt", [128, NT, H, 65], BF16)
    NMXQ = sb(es2, "NMXQ", [128, NT, H], F32)
    QSQ = sb(es2, "QSQ", [128, NT, H], F32)
    NMXK = sb(es2, "NMXK", [128, NT, H], F32)
    KSQ = sb(es2, "KSQ", [128, NT, H], F32)
    rawk = sb(es2, "rawk", [128, 4, M], BF16)
    vhat = sb(es2, "vhat", [128, 4, 65], BF16)
    vsum_r = sb(es2, "vsum_r", [1, H * 65], BF16)
    u_raw = sb(es2, "u_raw", [128, NT, H], F32)
    u_bf = sb(es2, "u_bf", [128, NT, H], BF16)
    urows = sb(es2, "urows", [16, NT, 128], BF16)

    E("vector", lambda v: v.memset(consts[:, 0:1], 1e-5), writes=["consts"])
    E("vector", lambda v: v.memset(consts[:, 1:2], -LN16), writes=["consts"])
    E("vector", lambda v: v.memset(onescol[:, :], 1.0), writes=["onescol"])
    E("vector", lambda v: v.memset(ones_row[:, :], 1.0), writes=["ones_row"])
    E("vector", lambda v: v.memset(onesblk[:, :], 0.0), writes=["onesblk"])
    E("vector", lambda v: v.memset(onesblk[0:64, 0:1], 1.0), writes=["onesblk"])
    E("vector", lambda v: v.memset(onesblk[64:128, 1:2], 1.0), writes=["onesblk"])
    E("gpsimd", lambda g: g.iota(iota_s[:, :], [[1, 128]], channel_multiplier=-1,
                                 allow_small_or_imprecise_dtypes=True), writes=["iota"])
    E("vector", lambda v: v.tensor_single_scalar(id128[:, :], iota_s[:, :], 0.0, OP.is_equal),
      reads=["iota"], writes=["id128"])
    E("vector", lambda v: v.tensor_single_scalar(id128f[:, :], iota_s[:, :], 0.0, OP.is_equal),
      reads=["iota"], writes=["id128f"])
    E("gpsimd", lambda g: g.dma_start(out=bo_row[:, :], in_=dbo), writes=["bo_row"], dma=True)
    E("gpsimd", lambda g: g.dma_start(out=projp[:, :, :], in_=dpj.rearrange("a p m -> p a m")),
      writes=["projp"], dma=True)
    E("gpsimd", lambda g: g.dma_start(
        out=w_aL[:, :, :], in_=dwq[:, 0:512].rearrange("(c p) e -> p c e", p=128)),
      writes=["w_aL"], dma=True)
    E("gpsimd", lambda g: g.dma_start(
        out=w_aH[:, :, :], in_=dwq[:, 512:1024].rearrange("(c p) e -> p c e", p=128)),
      writes=["w_aH"], dma=True)

    wq_tasks = []
    wbufi = [0]
    woff = 0
    secoff = 0
    for (dsrc, nrows, ncols) in ((dwg, IG, D), (dwu, IG, D), (dwd, DG, I)):
        secoff = {id(dwg): 0, id(dwu): IG * D, id(dwd): 2 * IG * D}[id(dsrc)]
        r0 = 0
        while r0 < nrows:
            rr = min(128, nrows - r0)
            c0 = 0
            while c0 < ncols:
                fc = min(QCH, ncols - c0)

                def wq_task(ds=dsrc, a=r0, b=rr, cc=c0, f=fc, sec=secoff,
                            nct=ncols // 128, il=nrows):
                    i = wbufi[0] % 2
                    wbufi[0] += 1
                    win, winN = wq_in_[i], "wq_in%d" % i
                    wout, woutN = wq_out_[i], "wq_out%d" % i
                    wtr, wtrN = wtr_[i], "wtr%d" % i
                    E("gpsimd", lambda g: g.dma_start(
                        out=win[0:b, 0:f], in_=ds[a:a + b, cc:cc + f]),
                      writes=[winN], dma="wqin_dma")
                    emit_quant_chunk(q1, win[0:b, 0:f], wout[0:b, 0:f], f,
                                     [winN], woutN, P=b)
                    ncl = f // 128
                    for cl in range(ncl):
                        E("tensor", lambda te, cl_=cl: te.transpose(
                            pbf[0:128, cl_ * 128:cl_ * 128 + b],
                            wout[0:b, cl_ * 128:(cl_ + 1) * 128], id128[0:b, 0:b]),
                          reads=[woutN, "id128"], writes=["pbf"])
                    E("vector", lambda v: v.tensor_copy(
                        wtr[:, 0:ncl * b].rearrange("p (c i) -> p c i", c=ncl),
                        pbf[0:128, 0:ncl * 128].rearrange(
                            "p (c i) -> p c i", c=ncl)[:, :, 0:b]),
                      reads=["pbf"], writes=[wtrN])
                    dst = cc_w_in.ap()[0:1, sec:sec + 128 * nct * il].rearrange(
                        "o (p c i) -> (o p) c i", p=128, c=nct)[
                            :, cc // 128:cc // 128 + ncl, a:a + b]
                    E("gpsimd", lambda g, d_=dst: g.dma_start(
                        out=d_, in_=wtr[:, 0:ncl * b].rearrange(
                            "p (c i) -> p c i", c=ncl)),
                      reads=[wtrN], writes=["ccw"], dma=True)
                wq_tasks.append(wq_task)
                woff += rr * fc
                c0 += fc
            r0 += rr

    for t in range(NT):
        xbuf = xs_[t % 2]
        xname = "xs%d" % (t % 2)
        E("sync", lambda g, xb=xbuf, tt=t: g.dma_start(
            out=xb[:, :], in_=dx[tt * 128:(tt + 1) * 128, :]), writes=[xname], dma=True)
        xnb = xn_s[:, t % 2, :]
        xnn = "xn%d" % (t % 2)
        emit_ln(sq1, xbuf, xname, xnb, xnn)
        E("scalar", lambda s, ob=xnb, tt=t: s.dma_start_transpose(
            X1T[:, :, tt * 128:(tt + 1) * 128], ob),
          reads=[xnn], writes=["X1T"], dma=True)
        for _ in range(2):
            if wq_tasks:
                wq_tasks.pop(0)()
    while wq_tasks:
        wq_tasks.pop(0)()

    for e in range(8):
        whalf, wname = (w_aL, "w_aL") if e < 4 else (w_aH, "w_aH")
        for g in range(4):
            bank = pb[(e * 4 + g) % 4]
            bname = "pb%d" % ((e * 4 + g) % 4)
            for c in range(8):
                E("tensor", lambda te, ee=e, gg=g, cc=c, wb=whalf, bk=bank: te.matmul(
                    bk[:, :], wb[:, cc, (ee % 4) * 128:(ee % 4 + 1) * 128],
                    X1T[:, cc, gg * 512:(gg + 1) * 512],
                    start=(cc == 0), stop=(cc == 7)),
                  reads=[wname, "X1T"] if c in (0, 7) else [],
                  writes=[bname] if c in (0, 7) else [])
            E("scalar", lambda s, ee=e, gg=g, bk=bank: s.activation(
                Qf[:, ee, gg * 512:(gg + 1) * 512], bk[:, :], AF.Identity, scale=0.125),
              reads=[bname], writes=["Qf"])

    E("gpsimd", lambda g: g.dma_start(
        out=w_aL[:, :, :], in_=dwv[:, 0:512].rearrange("(c p) e -> p c e", p=128)),
      writes=["w_aL"], dma=True)
    E("gpsimd", lambda g: g.dma_start(
        out=w_aH[:, :, :], in_=dwv[:, 512:1024].rearrange("(c p) e -> p c e", p=128)),
      writes=["w_aH"], dma=True)
    for t in range(NT):
        for g in range(2):
            whalf, wname = (w_aL, "w_aL") if g == 0 else (w_aH, "w_aH")
            bank = pb[(t * 2 + g) % 4]
            bname = "pb%d" % ((t * 2 + g) % 4)
            for c in range(8):
                E("tensor", lambda te, tt=t, cc=c, bk=bank, wb=whalf: te.matmul(
                    bk[:, :], X1T[:, cc, tt * 128:(tt + 1) * 128],
                    wb[:, cc, :],
                    start=(cc == 0), stop=(cc == 7)),
                  reads=[wname, "X1T"] if c in (0, 7) else [],
                  writes=[bname] if c in (0, 7) else [])
            E("vector", lambda v, tt=t, gg=g, bk=bank: v.tensor_copy(
                Vt[:, tt, gg * 8:(gg + 1) * 8, 0:64],
                bk[:, :].rearrange("p (h e) -> p h e", e=64)),
              reads=[bname], writes=["Vt"])
        E("vector", lambda v, tt=t: v.memset(Vt[:, tt, :, 64:65], 1.0), writes=["Vt"])

    dump("Qf", lambda: Qf[:, :, :], [128, 8, T], "Qf", BF16)
    dump("Vt", lambda: Vt[:, :, :, :], [128, NT, H, 65], "Vt", BF16)
    dump("X1T", lambda: X1T[:, :, :], [128, 8, T], "X1T", BF16)

    vw = ((pb[0], "pb0", 0, 512), (pb[1], "pb1", 512, 512), (pb[2], "pb2", 1024, 16))
    for (bank, bname, off, w) in vw:
        for t in range(NT):
            E("tensor", lambda te, tt=t, bk=bank, o=off, ww=w: te.matmul(
                bk[0:1, 0:ww], onescol[:, :],
                Vt[:, tt, :, :].rearrange("p h e -> p (h e)")[:, o:o + ww],
                start=(tt == 0), stop=(tt == NT - 1)),
              reads=["Vt", "onescol"] if t in (0, NT - 1) else [],
              writes=[bname] if t in (0, NT - 1) else [])
    E("vector", lambda v: v.tensor_single_scalar(vsum_r[0:1, 0:512], pb[0][0:1, :], E16, OP.mult),
      reads=["pb0"], writes=["vsum_r"])
    E("vector", lambda v: v.tensor_single_scalar(vsum_r[0:1, 512:1024], pb[1][0:1, :], E16, OP.mult),
      reads=["pb1"], writes=["vsum_r"])
    E("vector", lambda v: v.tensor_single_scalar(vsum_r[0:1, 1024:1040], pb[2][0:1, 0:16], E16, OP.mult),
      reads=["pb2"], writes=["vsum_r"])

    E("gpsimd", lambda g: g.dma_start(
        out=w_kh[:, :, :], in_=dwk[:, 0:512].rearrange("(c p) e -> p c e", p=128)),
      writes=["w_kh"], dma=True)
    for wave in range(4):
        heads = [wave * 4 + i for i in range(4)]
        kbuf = kfw[wave % 2]
        kname = "kfw%d" % (wave % 2)
        if wave == 2:
            E("gpsimd", lambda g: g.dma_start(
                out=w_kh[:, :, :], in_=dwk[:, 512:1024].rearrange("(c p) e -> p c e", p=128)),
              writes=["w_kh"], dma=True)
        for ei, e in enumerate((2 * wave, 2 * wave + 1)):
            for g in range(4):
                bank = pb[4 + (g % 2)]
                bname = "pb%d" % (4 + (g % 2))
                for c in range(8):
                    E("tensor", lambda te, ee=e, gg=g, cc=c, bk=bank: te.matmul(
                        bk[:, :], w_kh[:, cc, (ee % 4) * 128:(ee % 4 + 1) * 128],
                        X1T[:, cc, gg * 512:(gg + 1) * 512],
                        start=(cc == 0), stop=(cc == 7)),
                      reads=["w_kh", "X1T"] if c in (0, 7) else [],
                      writes=[bname] if c in (0, 7) else [])
                E("scalar", lambda s, ei_=ei, gg=g, bk=bank, kb_=kbuf: s.activation(
                    kb_[:, ei_, gg * 512:(gg + 1) * 512], bk[:, :], AF.Identity, scale=0.125),
                  reads=[bname], writes=[kname])
        for ei in range(2):
            E("scalar", lambda s, kb_=kbuf, ei_=ei: s.activation(
                sq1[:, :], kb_[:, ei_, :], AF.Square),
              reads=[kname], writes=["sq_scr"])
            cglob = 2 * wave + ei
            for t in range(NT):
                E("tensor", lambda te, tt=t, cg=cglob: te.matmul(
                    pb7[:, tt * 16 + 2 * cg: tt * 16 + 2 * cg + 2],
                    sq1[:, tt * 128:(tt + 1) * 128], onesblk[:, :],
                    start=True, stop=True),
                  reads=["sq_scr", "onesblk"] if t in (0, NT - 1) else [],
                  writes=["pb7"] if ((t == 0 and ei == 0) or (t == NT - 1 and ei == 1)) else [])
        E("vector", lambda v, w_=wave: v.tensor_copy(
            KSQ[:, :, 4 * w_:4 * w_ + 4],
            pb7[:, 0:256].rearrange("p (t h) -> p t h", h=16)[:, :, 4 * w_:4 * w_ + 4]),
          reads=["pb7"], writes=["KSQ"])
        XWB = (pb[4], pb[5], pb7)
        XWN = ("pb4", "pb5", "pb7")

        def kw_xw(k, t, sub):
            hh2 = heads[2 * sub:2 * sub + 2]
            for j, h in enumerate(hh2):
                i = 2 * sub + j
                bank = XWB[(2 * k + j) % 3]
                bname = XWN[(2 * k + j) % 3]
                hr = (h % 2) * 64
                ei = i // 2
                hp = 2 * wave + i // 2
                E("tensor", lambda te, bk=bank, kb_=kbuf, ei_=ei, hr_=hr, hp_=hp, tt=t: te.matmul(
                    bk[:, 0:256],
                    kb_[hr_:hr_ + 64, ei_, tt * 128:(tt + 1) * 128],
                    projp[hr_:hr_ + 64, hp_, :],
                    start=True, stop=True, tile_position=(hr_, 0)),
                  reads=[kname, "projp"], writes=[bname])

        def kw_tail(k, t, sub):
            hh2 = heads[2 * sub:2 * sub + 2]
            for j, h in enumerate(hh2):
                bname = XWN[(2 * k + j) % 3]
                E("vector", lambda v, bk=XWB[(2 * k + j) % 3], h_=h, tt=t: v.tensor_reduce(
                    NMXK[:, tt, h_:h_ + 1],
                    bk[:, 0:256], AX.X, OP.max, negate=True),
                  reads=[bname], writes=["NMXK"])
            E("vector", lambda v, h0=hh2[0], tt=t, sb=sub: v.scalar_tensor_tensor(
                sm[:, 12 + 2 * sb:14 + 2 * sb], KSQ[:, tt, h0:h0 + 2], -0.5,
                NMXK[:, tt, h0:h0 + 2], OP.mult, OP.add),
              reads=["KSQ", "NMXK"], writes=["smk%d" % sub])
            E("scalar", lambda s, sb=sub: s.activation(
                sm[:, 16 + 2 * sb:18 + 2 * sb], sm[:, 12 + 2 * sb:14 + 2 * sb],
                AF.Exp, bias=consts[:, 1:2]),
              reads=["smk%d" % sub, "consts"], writes=["smk2%d" % sub])
            for j, h in enumerate(hh2):
                i = 2 * sub + j
                bname = XWN[(2 * k + j) % 3]
                E("scalar", lambda s, bk=XWB[(2 * k + j) % 3], i_=i: s.activation(
                    rawk[:, i_, :], bk[:, 0:256], AF.Exp),
                  reads=[bname], writes=["rawk%d" % i])

        def kw_tail2(t, sub):
            hh2 = heads[2 * sub:2 * sub + 2]
            for j, h in enumerate(hh2):
                i = 2 * sub + j
                E("vector", lambda v, i_=i, h_=h, tt=t, sb=sub: v.tensor_scalar(
                    vhat[:, i_, :], Vt[:, tt, h_, :], sm[:, 16 + i_:17 + i_], 1.0,
                    OP.mult, OP.mult),
                  reads=["Vt", "smk2%d" % sub], writes=["vhat%d" % i])
            for j, h in enumerate(hh2):
                i = 2 * sub + j
                bank = pb[i]
                bname = "pb%d" % i
                E("tensor", lambda te, i_=i, bk=bank, tt=t: te.matmul(
                    bk[0:65, 0:256], vhat[:, i_, :], rawk[:, i_, :],
                    start=(tt == 0), stop=False),
                  reads=["vhat%d" % i, "rawk%d" % i],
                  writes=[bname] if t == 0 else [])

        items = [(t, sub) for t in range(NT) for sub in range(2)]
        kw_xw(0, *items[0])
        prev = None
        for k, it in enumerate(items):
            kw_tail(k, *it)
            if k + 1 < len(items):
                kw_xw(k + 1, *items[k + 1])
            if prev is not None:
                kw_tail2(*prev)
            prev = it
        kw_tail2(*prev)
        for i, h in enumerate(heads):
            bank = pb[i]
            bname = "pb%d" % i
            E("tensor", lambda te, bk=bank, h_=h: te.matmul(
                bk[0:65, 0:256], vsum_r[0:1, h_ * 65:(h_ + 1) * 65],
                ones_row[0:1, :], start=False, stop=True),
              reads=["vsum_r", "ones_row"], writes=[bname])
            E("vector", lambda v, bk=bank, i_=i: v.tensor_copy(
                kvst[:, i_, :], bk[0:65, 0:256]),
              reads=[bname], writes=["kvst"])
        E("gpsimd", lambda g, w_=wave: g.dma_start(
            out=cc_kv_in.ap()[4 * w_:4 * w_ + 4, :, :].rearrange("h d m -> d h m"),
            in_=kvst[:, :, :]),
          reads=["kvst"], writes=["cckv"], dma=True)

    dump("KSQ", lambda: KSQ[:, :, :], [128, NT, H], "KSQ")
    dump("NMXK", lambda: NMXK[:, :, :], [128, NT, H], "NMXK")
    dump("kvin", lambda: cc_kv_in.ap(), [H, 65, M], "cckv")
    E("gpsimd", lambda g: g.collective_compute(
        "AllReduce", OP.add, replica_groups=[[0, 1, 2, 3], [4, 5, 6, 7]],
        ins=[cc_kv_in.ap().opt()], outs=[cc_kv_out.ap().opt()]),
      reads=["cckv"], writes=["cckv_out"], cc=True)

    for c in range(8):
        E("scalar", lambda s, cc=c: s.activation(sq1[:, :], Qf[:, cc, :], AF.Square),
          reads=["Qf"], writes=["sq_scr"])
        for t in range(NT):
            E("tensor", lambda te, tt=t, cc=c: te.matmul(
                pb7[:, tt * 16 + 2 * cc: tt * 16 + 2 * cc + 2],
                sq1[:, tt * 128:(tt + 1) * 128], onesblk[:, :],
                start=True, stop=True),
              reads=["sq_scr", "onesblk"] if t in (0, NT - 1) else [],
              writes=["pb7"] if ((t == 0 and c == 0) or (t == NT - 1 and c == 7)) else [])
    E("vector", lambda v: v.tensor_copy(
        QSQ[:, :, :].rearrange("p t h -> p (t h)"), pb7[:, 0:256]),
      reads=["pb7"], writes=["QSQ"])

    for wave in range(4):
        heads = [wave * 4 + i for i in range(4)]
        XWB = (0, 1, 4, 5)
        for t in range(NT):
            for i, h in enumerate(heads):
                bank = pb[XWB[i]]
                bname = "pb%d" % XWB[i]
                hp, hr = h // 2, (h % 2) * 64
                E("tensor", lambda te, bk=bank, hp_=hp, hr_=hr, tt=t: te.matmul(
                    bk[:, 0:256],
                    Qf[hr_:hr_ + 64, hp_, tt * 128:(tt + 1) * 128],
                    projp[hr_:hr_ + 64, hp_, :],
                    start=True, stop=True, tile_position=(hr_, 0)),
                  reads=["Qf", "projp"], writes=[bname])
            for i, h in enumerate(heads):
                bname = "pb%d" % XWB[i]
                E("vector", lambda v, bk=pb[XWB[i]], h_=h, tt=t: v.tensor_reduce(
                    NMXQ[:, tt, h_:h_ + 1],
                    bk[:, 0:256], AX.X, OP.max, negate=True),
                  reads=[bname], writes=["NMXQ"])

    E("vector", lambda v: v.scalar_tensor_tensor(
        u_raw[:, :, :].rearrange("p t h -> p (t h)"),
        QSQ[:, :, :].rearrange("p t h -> p (t h)"), 0.5,
        NMXQ[:, :, :].rearrange("p t h -> p (t h)"), OP.mult, OP.subtract),
      reads=["QSQ", "NMXQ"], writes=["u_raw"])
    E("scalar", lambda s: s.activation(
        u_bf[:, :, :].rearrange("p t h -> p (t h)"),
        u_raw[:, :, :].rearrange("p t h -> p (t h)"), AF.Exp),
      reads=["u_raw"], writes=["u_bf"])
    E("scalar", lambda s: s.activation(
        u_raw[:, :, :].rearrange("p t h -> p (t h)"),
        u_raw[:, :, :].rearrange("p t h -> p (t h)"), AF.Exp),
      reads=["u_raw"], writes=["u_raw"])
    for t in range(NT):
        E("tensor", lambda te, tt=t: te.transpose(
            pbf[0:16, 0:128], u_bf[:, tt, :], id128[:, :]),
          reads=["u_bf", "id128"], writes=["pbf"])
        E("vector", lambda v, tt=t: v.tensor_copy(urows[:, tt, :], pbf[0:16, 0:128]),
          reads=["pbf"], writes=["urows"])

    dump("NMXQ", lambda: NMXQ[:, :, :], [128, NT, H], "NMXQ")
    dump("u_raw", lambda: u_raw[:, :, :], [128, NT, H], "u_raw")
    dump("u_bf", lambda: u_bf[:, :, :], [128, NT, H], "u_bf", BF16)
    dump("urows", lambda: urows[:, :, :], [16, NT, 128], "urows", BF16)
    S.barrier()
    S.replay_new(block, sems)
    es1.close()

    es3 = ExitStack()
    w_b = sb(es3, "w_b", [128, 8, D], BF16, side="right")
    kvb = sb(es3, "kvb", [65, H, M], BF16, side="right")
    Ccol = sb(es3, "Ccol", [65, H], F32, side="right")
    ccol_bf = sb(es3, "ccol_bf", [65, H], BF16, side="right")
    ct_sb = sb(es3, "ct_sb", [16, 65], BF16, side="right")
    cflat = sb(es3, "cflat", [1, 16 * 65], BF16, side="right")
    kvt = sb(es3, "kvt", [128, 2, H, 65], BF16, side="right")
    dstg = [sb(es3, "dstg0", [1, 512], F32, side="right")]
    rawq = sb(es3, "rawq", [128, 2, 2, T], BF16, side="right")
    Mg = sb(es3, "Mg", [128, 8, T], BF16, side="right")
    denF = sb(es3, "denF", [16, T], F32, side="right")
    denT = sb(es3, "denT", [128, NT, H], F32, side="right")
    r_bf = sb(es3, "r_bf", [128, NT, H], BF16, side="right")
    rrows = sb(es3, "rrows", [16, NT, 128], BF16, side="right")
    uflat2 = sb(es3, "uflat2", [1, 2 * T], BF16, side="right")
    rflat2 = None
    srep = sb(es3, "srep", [128, H], F32, side="right")
    rbc = sb(es3, "rbc", [128, 512], F32, side="right")

    E("gpsimd", lambda g: g.dma_start(
        out=w_b[:, :, :], in_=dwo.rearrange("(c p) e -> p c e", p=128)),
      writes=["w_b"], dma=True)
    E("gpsimd", lambda g: g.dma_start(
        out=kvb[:, :, :], in_=cc_kv_out.ap().rearrange("h d m -> d h m")),
      reads=["cckv_out"], writes=["kvb"], dma=True)
    E("vector", lambda v: v.nop(), reads=["kvb", "w_b"])
    E("gpsimd", lambda g: g.collective_compute(
        "AllGather", OP.bypass, replica_groups=[list(range(8))],
        ins=[cc_w_in.ap().opt()], outs=[cc_w_out.ap().opt()]),
      reads=["ccw"], writes=["ccw_out"], cc=True)
    E("vector", lambda v: v.tensor_reduce(Ccol[:, :], kvb[:, :, :], AX.X, OP.add),
      reads=["kvb"], writes=["Ccol"])
    E("vector", lambda v: v.tensor_single_scalar(Ccol[64:65, :], Ccol[64:65, :], 16.0, OP.add),
      writes=["Ccol"])
    E("vector", lambda v: v.tensor_single_scalar(Ccol[:, :], Ccol[:, :], 16.0 * E16, OP.mult),
      writes=["Ccol"])
    E("vector", lambda v: v.tensor_copy(ccol_bf[:, :], Ccol[:, :]),
      reads=["Ccol"], writes=["ccol_bf"])
    E("tensor", lambda te: te.transpose(pbf[0:16, 0:65], ccol_bf[:, :], id128[0:65, 0:65]),
      reads=["ccol_bf", "id128"], writes=["pbf"])
    E("vector", lambda v: v.tensor_copy(ct_sb[:, :], pbf[0:16, 0:65]),
      reads=["pbf"], writes=["ct_sb"])
    E("scalar", lambda s: s.dma_start(
        out=cflat[0:1, :].rearrange("o (h e) -> o h e", h=16), in_=ct_sb[:, :]),
      reads=["ct_sb", "trq"], writes=["cflat", "trq"], dma=True)
    for h in range(H):
        for c in range(2):
            E("tensor", lambda te, h_=h, c_=c: te.transpose(
                pbf[0:128, 0:65], kvb[:, h_, c_ * 128:(c_ + 1) * 128], id128[0:65, 0:65]),
              reads=["kvb", "id128"], writes=["pbf"])
            E("vector", lambda v, h_=h, c_=c: v.tensor_copy(kvt[:, c_, h_, :], pbf[0:128, 0:65]),
              reads=["pbf"], writes=["kvt"])
    E("tensor", lambda te: te.matmul(
        pb7[:, 0:16], ones_row[0:1, 0:128],
        cflat[0:1, :].rearrange("o (h e) -> o h e", e=65)[:, :, 64:65]
            .rearrange("o h z -> o (h z)"),
        start=True, stop=True),
      reads=["cflat", "ones_row"], writes=["pb7"])
    E("vector", lambda v: v.tensor_copy(srep[:, :], pb7[:, 0:16]),
      reads=["pb7"], writes=["srep"])

    for p in range(8):
        hA, hB = 2 * p, 2 * p + 1
        E("scalar", lambda s, pp=p: s.dma_start(
            out=uflat2[0:1, :].rearrange("o (i t k) -> o i t k", i=2, t=NT),
            in_=urows[2 * pp:2 * pp + 2, :, :]),
          reads=["urows", "trq"], writes=["uflat2", "trq"], dma=True)
        for k, (c, i, g) in enumerate(
                [(c, i, g) for c in range(2) for i in range(2) for g in range(4)]):
            hr = (0, 64)[i]
            bank = (pb[0], pb[1], pb7)[k % 3]
            bname = ("pb0", "pb1", "pb7")[k % 3]
            E("tensor", lambda te, bk=bank, c_=c, hr_=hr, pp=p, gg=g: te.matmul(
                bk[:, 0:512],
                projp[hr_:hr_ + 64, pp, c_ * 128:(c_ + 1) * 128],
                Qf[hr_:hr_ + 64, pp, gg * 512:(gg + 1) * 512],
                start=True, stop=True, tile_position=(hr_, 0)),
              reads=["projp", "Qf"], writes=[bname])
            E("scalar", lambda s, bk=bank, c_=c, i_=i, gg=g: s.activation(
                rawq[:, c_, i_, gg * 512:(gg + 1) * 512], bk[:, 0:512], AF.Exp),
              reads=[bname], writes=["rawq"])
        for g in range(4):
            bank = pb[2 + g]
            bname = "pb%d" % (2 + g)
            for i in range(2):
                h_ = (hA, hB)[i]
                E("tensor", lambda te, bk=bank, i_=i, h__=h_, gg=g: te.matmul(
                    bk[0:65, :],
                    kvt[:, 0, h__, 0:65],
                    rawq[:, 0, i_, gg * 512:(gg + 1) * 512],
                    start=True, stop=False),
                  reads=["kvt", "rawq"], writes=[bname])
                E("tensor", lambda te, bk=bank, i_=i, h__=h_, gg=g: te.matmul(
                    bk[0:64, :],
                    cflat[0:1, h__ * 65:h__ * 65 + 64],
                    uflat2[0:1, i_ * T + gg * 512:i_ * T + (gg + 1) * 512],
                    start=False, stop=False),
                  reads=["cflat", "uflat2"])
                E("tensor", lambda te, bk=bank, i_=i, h__=h_, gg=g: te.matmul(
                    bk[0:65, :],
                    kvt[:, 1, h__, 0:65],
                    rawq[:, 1, i_, gg * 512:(gg + 1) * 512],
                    start=False, stop=True),
                  reads=["kvt", "rawq"], writes=[bname])
                E("vector", lambda v, bk=bank, pp=p, i_=i, gg=g: v.tensor_copy(
                    Mg[64 * i_:64 * i_ + 64, pp, gg * 512:(gg + 1) * 512],
                    bk[0:64, :]),
                  reads=[bname], writes=["Mg"])
                ds = dstg[0]
                dsn = "dstg0"
                E("vector", lambda v, bk=bank, d_=ds: v.tensor_copy(
                    d_[0:1, :], bk[64:65, :]),
                  reads=[bname], writes=[dsn])
                E("sync", lambda gq, d_=ds, h__=h_, gg=g: gq.dma_start(
                    out=denF[h__:h__ + 1, gg * 512:(gg + 1) * 512], in_=d_[0:1, :]),
                  reads=[dsn], writes=["denF"], dma=True)

    dump("kvb", lambda: kvb[:, :, :], [65, H, M], "kvb", BF16)
    dump("cflat", lambda: cflat[0:1, :], [1, 16 * 65], "cflat", BF16)
    dump("kvt", lambda: kvt[:, :, :, :], [128, 2, H, 65], "kvt", BF16)
    dump("MgN", lambda: Mg[:, :, :], [128, 8, T], "Mg", BF16)

    for t in range(NT):
        E("tensor", lambda te, tt=t: te.transpose(
            pb7[0:128, 0:16], denF[:, tt * 128:(tt + 1) * 128], id128f[0:16, 0:16]),
          reads=["denF", "id128f"], writes=["pb7"])
        E("vector", lambda v, tt=t: v.tensor_copy(denT[:, tt, :], pb7[0:128, 0:16]),
          reads=["pb7"], writes=["denT"])
    for t in range(NT):
        E("vector", lambda v, tt=t: v.tensor_tensor(
            sm[:, 0:16], u_raw[:, tt, :], srep[:, :], OP.mult),
          reads=["u_raw", "srep"], writes=["sm"])
        E("vector", lambda v, tt=t: v.tensor_tensor(
            denT[:, tt, :], denT[:, tt, :], sm[:, 0:16], OP.add),
          reads=["sm"], writes=["denT"])
    E("vector", lambda v: v.reciprocal(
        denT[:, :, :].rearrange("p t h -> p (t h)"),
        denT[:, :, :].rearrange("p t h -> p (t h)")), writes=["denT"])
    E("vector", lambda v: v.tensor_copy(
        r_bf[:, :, :].rearrange("p t h -> p (t h)"),
        denT[:, :, :].rearrange("p t h -> p (t h)")),
      reads=["denT"], writes=["r_bf"])
    for t in range(NT):
        E("tensor", lambda te, tt=t: te.transpose(
            pbf[0:16, 0:128], r_bf[:, tt, :], id128[:, :]),
          reads=["r_bf", "id128"], writes=["pbf"])
        E("vector", lambda v, tt=t: v.tensor_copy(rrows[:, tt, :], pbf[0:16, 0:128]),
          reads=["pbf"], writes=["rrows"])

    dump("denF", lambda: denF[:, :], [16, T], "denF", F32)
    dump("denT", lambda: denT[:, :, :], [128, NT, H], "denT")

    for p in range(8):
        hA, hB = 2 * p, 2 * p + 1
        E("scalar", lambda s, pp=p: s.dma_start(
            out=uflat2[0:1, :].rearrange("o (i t k) -> o i t k", i=2, t=NT),
            in_=rrows[2 * pp:2 * pp + 2, :, :]),
          reads=["rrows", "trq"], writes=["uflat2", "trq"], dma=True)
        for g in range(4):
            for i, pbase in enumerate((0, 64)):
                E("tensor", lambda te, pb_=pbase, i_=i, gg=g: te.matmul(
                    pb7[pb_:pb_ + 64, :],
                    ones_row[0:1, 0:64],
                    uflat2[0:1, i_ * T + gg * 512:i_ * T + (gg + 1) * 512],
                    start=True, stop=True, tile_position=(0, pb_)),
                  reads=["uflat2", "ones_row"], writes=["pb7"])
            E("scalar", lambda s: s.copy(rbc[:, :], pb7[:, :]),
              reads=["pb7"], writes=["rbc"])
            E("vector", lambda v, pp=p, gg=g: v.tensor_tensor(
                Mg[:, pp, gg * 512:(gg + 1) * 512],
                Mg[:, pp, gg * 512:(gg + 1) * 512],
                rbc[:, :], OP.mult),
              reads=["rbc"], writes=["Mg"])

    dump("Mg", lambda: Mg[:, :, :], [128, 8, T], "Mg", BF16)

    for t in range(NT):
        xbuf = xs_[t % 2]
        xname = "xs%d" % (t % 2)
        E("sync", lambda g, xb=xbuf, tt=t: g.dma_start(
            out=xb[:, :], in_=dx[tt * 128:(tt + 1) * 128, :]), writes=[xname], dma=True)
        for g in range(2):
            bank = pb[g]
            bname = "pb%d" % g
            for c in range(8):
                E("tensor", lambda te, bk=bank, cc=c, tt=t, gg=g: te.matmul(
                    bk[:, :], Mg[:, cc, tt * 128:(tt + 1) * 128],
                    w_b[:, cc, gg * 512:(gg + 1) * 512],
                    start=(cc == 0), stop=False),
                  reads=["Mg", "w_b"] if c in (0, 7) else [],
                  writes=[bname] if c == 0 else [])
            E("tensor", lambda te, bk=bank, gg=g: te.matmul(
                bk[:, :], ones_row[0:1, 0:128], bo_row[0:1, gg * 512:(gg + 1) * 512],
                start=False, stop=True),
              reads=["bo_row", "ones_row"], writes=[bname])
            E("vector", lambda v, xb=xbuf, bk=bank, gg=g: v.tensor_tensor(
                xb[:, gg * 512:(gg + 1) * 512], xb[:, gg * 512:(gg + 1) * 512],
                bk[:, :], OP.add),
              reads=[bname, xname], writes=[xname])
        E("sync", lambda g, xb=xbuf, tt=t: g.dma_start(
            out=d_x1.ap()[tt * 128:(tt + 1) * 128, :], in_=xb[:, :]),
          reads=[xname], writes=["x1_%d" % t], dma=True)

    dump("x1", lambda: d_x1.ap(), [T, D], "x1_%d" % (NT - 1))
    S.barrier()
    S.replay_new(block, sems)
    es2.close()
    es3.close()

    es4 = ExitStack()
    wgT = sb(es4, "wgT", [128, 8, I], BF16)
    wuT = sb(es4, "wuT", [128, 8, I], BF16)
    wdT = sb(es4, "wdT", [128, 22, D], BF16)
    x2t_ = [sb(es4, "x2t%d" % i, [128, 8, 128], BF16) for i in range(2)]
    xq2_ = [sb(es4, "xq2%d" % i, [128, D], BF16) for i in range(2)]
    xn2_ = [sb(es4, "xn2%d" % i, [128, D], F32) for i in range(2)]
    hch_ = [sb(es4, "hch%d" % i, [128, 512], F32) for i in range(2)]
    hqc_ = [sb(es4, "hqc%d" % i, [128, 512], BF16) for i in range(3)]
    hqT_ = [sb(es4, "hqT%d" % i, [128, 4, 128], BF16) for i in range(3)]
    gs_ = [sb(es4, "gs%d" % i, [128, 512], F32) for i in range(2)]
    sq3 = sb(es4, "sq3", [128, D], BF16)
    q3 = mkquant(es4, tag="3")

    for r in range(8):
        for (dstT, dname, nrows, ncols, sec, qeng) in (
                (wgT, "wgT", IG, D, 0, "sync"),
                (wuT, "wuT", IG, D, IG * D, "gpsimd"),
                (wdT, "wdT", DG, I, 2 * IG * D, "sync")):
            nct = ncols // 128
            src = cc_w_out.ap()[r:r + 1, sec:sec + 128 * nct * nrows].rearrange(
                "o (p c i) -> (o p) c i", p=128, c=nct)
            E(qeng, lambda s, sr=src, dT=dstT, a0=r * nrows, a1=(r + 1) * nrows:
                s.dma_start(out=dT[:, :, a0:a1], in_=sr),
              reads=["ccw_out"], writes=[dname],
              dma=("w_dma" if qeng == "sync" else True))
    dump("wgTd", lambda: wgT[:, :, :], [128, 8, I], "wgT", BF16)
    dump("wuTd", lambda: wuT[:, :, :], [128, 8, I], "wuT", BF16)
    dump("wdTd", lambda: wdT[:, :, :], [128, 22, D], "wdT", BF16)

    NGRP = 6

    def ffn_down_chunk(t, g):
        wl = 512 if g < 5 else 256
        hqT = hqT_[g % 3]
        hqTn = "hqT%d" % (g % 3)
        ncl = wl // 128
        for cl in range(ncl):
            cc = g * 4 + cl
            for half in range(2):
                bank = pb[4 + half]
                bname = "pb%d" % (4 + half)
                first = (g == 0 and cl == 0)
                last = (g == NGRP - 1 and cl == ncl - 1)
                E("tensor", lambda te, bk=bank, hT=hqT, cl_=cl, cc_=cc, hf=half,
                    f=first, l=last: te.matmul(
                    bk[:, :], hT[:, cl_, :], wdT[:, cc_, hf * 512:(hf + 1) * 512],
                    start=f, stop=l),
                  reads=[hqTn, "wdT"],
                  writes=[bname] if (first or last) else [])

    def ffn_residual_out(t):
        xbuf = xs_[t % 2]
        xname = "xs%d" % (t % 2)
        for half in range(2):
            bank = pb[4 + half]
            bname = "pb%d" % (4 + half)
            E("vector", lambda v, xb=xbuf, bk=bank, hf=half: v.tensor_tensor(
                xb[:, hf * 512:(hf + 1) * 512], xb[:, hf * 512:(hf + 1) * 512],
                bk[:, :], OP.add),
              reads=[bname, xname], writes=[xname])
        E("gpsimd", lambda g, xb=xbuf, tt=t: g.dma_start(
            out=dout[tt * 128:(tt + 1) * 128, :], in_=xb[:, :]),
          reads=[xname], writes=["out_%d" % t], dma=True)

    def ffn_front(t):
        xbuf = xs_[t % 2]
        xname = "xs%d" % (t % 2)
        xn2 = xn2_[t % 2]
        xn2n = "xn2%d" % (t % 2)
        xq2 = xq2_[t % 2]
        xq2n = "xq2%d" % (t % 2)
        x2t = x2t_[t % 2]
        x2tn = "x2t%d" % (t % 2)
        E("gpsimd", lambda g, xb=xbuf, tt=t: g.dma_start(
            out=xb[:, :], in_=d_x1.ap()[tt * 128:(tt + 1) * 128, :]),
          reads=["x1_%d" % t], writes=[xname], dma=True)
        emit_ln(sq3, xbuf, xname, xn2[:, :], xn2n)
        emit_quant(q3, lambda c0, fc: xn2[:, c0:c0 + fc],
                   lambda c0, fc: xq2[:, c0:c0 + fc], D, [xn2n], xq2n)
        E("sync", lambda s, xt=x2t, xq=xq2: s.dma_start_transpose(
            xt[:, :, :], xq[:, :]),
          reads=[xq2n], writes=[x2tn], dma="x2t_dma")

    for t in range(NT):
        x2t = x2t_[t % 2]
        x2tn = "x2t%d" % (t % 2)
        ffn_front(t)
        if t >= 1:
            ffn_residual_out(t - 1)
        for g in range(NGRP):
            w0 = g * 512
            wlen = 512 if g < 5 else 256
            bkg = pb[(2 * g) % 4]
            bng = "pb%d" % ((2 * g) % 4)
            bku = pb[(2 * g + 1) % 4]
            bnu = "pb%d" % ((2 * g + 1) % 4)
            for c in range(8):
                E("tensor", lambda te, bk=bkg, xt=x2t, cc=c, w0_=w0, wl=wlen: te.matmul(
                    bk[:, 0:wl], xt[:, cc, :], wgT[:, cc, w0_:w0_ + wl],
                    start=(cc == 0), stop=(cc == 7)),
                  reads=[x2tn, "wgT"] if c in (0, 7) else [],
                  writes=[bng] if c in (0, 7) else [])
            for c in range(8):
                E("tensor", lambda te, bk=bku, xt=x2t, cc=c, w0_=w0, wl=wlen: te.matmul(
                    bk[:, 0:wl], xt[:, cc, :], wuT[:, cc, w0_:w0_ + wl],
                    start=(cc == 0), stop=(cc == 7)),
                  reads=[x2tn, "wuT"] if c in (0, 7) else [],
                  writes=[bnu] if c in (0, 7) else [])
            if g >= 2:
                ffn_down_chunk(t, g - 2)
            gsb, gsn = gs_[g % 2], "gs%d" % (g % 2)
            hch, hchn = hch_[g % 2], "hch%d" % (g % 2)
            hqc, hqcn = hqc_[g % 3], "hqc%d" % (g % 3)
            hqT, hqTn = hqT_[g % 3], "hqT%d" % (g % 3)
            E("scalar", lambda s, bk=bkg, gb=gsb, wl=wlen: s.activation(
                gb[:, 0:wl], bk[:, 0:wl], AF.Silu),
              reads=[bng], writes=[gsn])
            E("vector", lambda v, bk=bku, hb=hch, gb=gsb, wl=wlen: v.tensor_tensor(
                hb[:, 0:wl], bk[:, 0:wl], gb[:, 0:wl], OP.mult),
              reads=[bnu, gsn], writes=[hchn])
            emit_quant_chunk(q3, hch[0:128, 0:wlen], hqc[0:128, 0:wlen], wlen,
                             [hchn], hqcn)
            E("sync", lambda s, hT=hqT, hb=hqc, wl=wlen: s.dma_start_transpose(
                hT[:, 0:wl // 128, :], hb[0:128, 0:wl]),
              reads=[hqcn], writes=[hqTn], dma="hqt_dma")
        ffn_down_chunk(t, NGRP - 2)
        ffn_down_chunk(t, NGRP - 1)

    ffn_residual_out(NT - 1)
    dump("xn20d", lambda: xn2_[0][:, :], [128, D], "xn20")
    dump("xq20d", lambda: xq2_[0][:, :], [128, D], "xq20", BF16)
    dump("x2t0d", lambda: x2t_[0][:, :, :], [128, 8, 128], "x2t0", BF16)
    dump("hqT0d", lambda: hqT_[0][:, :, :], [128, 4, 128], "hqT0", BF16)
    dump("hqT1d", lambda: hqT_[1][:, :, :], [128, 4, 128], "hqT1", BF16)
    E("gpsimd", lambda g: g.nop(), reads=["out_%d" % (NT - 1)],
      extra_waits=[("gpsimd_dma", S.count["gpsimd_dma"])])
    S.replay_new(block, sems)
    es4.close()

    block_cm.__exit__(None, None, None)
    es0.close()
    return nc


_NC_CACHE = {}


def kernel(L0_attn_x, L0_attn_Wq, L0_attn_Wk, L0_attn_Wv, L0_attn_Wo, L0_attn_bo,
           L0_attn_proj, L0_ffn_w_gate, L0_ffn_w_up, L0_ffn_w_down):
    x = np.ascontiguousarray(np.asarray(L0_attn_x, dtype=np.float32))
    wqt = np.ascontiguousarray(np.asarray(L0_attn_Wq, np.float32).T)
    wkt = np.ascontiguousarray(np.asarray(L0_attn_Wk, np.float32).T)
    wvt = np.ascontiguousarray(np.asarray(L0_attn_Wv, np.float32).T)
    wot = np.ascontiguousarray(np.asarray(L0_attn_Wo, np.float32).T)
    bo = np.ascontiguousarray(np.asarray(L0_attn_bo, np.float32).reshape(1, D))
    pj = np.asarray(L0_attn_proj, np.float32).transpose(0, 2, 1)
    projp = np.ascontiguousarray(pj.reshape(8, 128, M))
    wg = np.asarray(L0_ffn_w_gate, np.float32)
    wu = np.asarray(L0_ffn_w_up, np.float32)
    wd = np.asarray(L0_ffn_w_down, np.float32)

    if "nc" not in _NC_CACHE:
        _NC_CACHE["nc"] = build_nc()
    nc = _NC_CACHE["nc"]

    in_maps = []
    for c in range(NCORE):
        b, s = c // 4, c % 4
        in_maps.append({
            "x": np.ascontiguousarray(x[b, s * T:(s + 1) * T, :]),
            "wqt": wqt, "wkt": wkt, "wvt": wvt, "wot": wot,
            "bo": bo, "projp": projp,
            "wg_s": np.ascontiguousarray(wg[c * IG:(c + 1) * IG, :]),
            "wu_s": np.ascontiguousarray(wu[c * IG:(c + 1) * IG, :]),
            "wd_s": np.ascontiguousarray(wd[c * DG:(c + 1) * DG, :]),
        })
    res = run_bass_kernel_spmd(nc, in_maps, core_ids=list(range(NCORE)))
    out = np.empty((B, TFULL, D), np.float32)
    for c in range(NCORE):
        b, s = c // 4, c % 4
        out[b, s * T:(s + 1) * T, :] = res.results[c]["out"]
    return out


# version 30
# speedup vs baseline: 1.0601x; 1.0601x over previous
import math
import numpy as np
import ml_dtypes
from contextlib import ExitStack

import concourse.bass as bass
import concourse.mybir as mybir
from concourse.bass_utils import run_bass_kernel_spmd
from concourse.alu_op_type import AluOpType as OP

F32 = mybir.dt.float32
BF16 = mybir.dt.bfloat16
BF = ml_dtypes.bfloat16
AF = mybir.ActivationFunctionType
AX = mybir.AxisListType

B, TFULL, D = 2, 8192, 1024
NCORE = 8
T = 2048
NT = T // 128
H, M, DK = 16, 256, 64
I = 2816
IG = I // 8
DG = D // 8
EPS = 1e-6
E16 = EPS / 16.0
LN16 = math.log(16.0)
QCH = 512
DEBUG_DUMPS = False


class Sched:
    ENGS = ("sync", "gpsimd", "vector", "scalar", "tensor")

    def __init__(self):
        self.ops = []
        self.mark = 0
        self.count = {e: 0 for e in self.ENGS}
        for e in self.ENGS:
            self.count[e + "_dma"] = 0
            self.count[e + "_echo"] = 0
        self.count["cc"] = 0
        self.count["x2t_dma"] = 0
        self.count["hqt_dma"] = 0
        self.count["w_dma"] = 0
        self.count["wqin_dma"] = 0
        self.last_w = {}
        self.readers = {}
        self.echo = {}

    QSRC = {"sync_dma": "sync", "gpsimd_dma": "gpsimd", "scalar_dma": "scalar",
            "vector_dma": "vector", "tensor_dma": "tensor", "cc": "gpsimd",
            "x2t_dma": "sync", "hqt_dma": "sync", "w_dma": "sync",
            "wqin_dma": "gpsimd"}

    def _echo(self, ws):
        cum = self.count[ws]
        last_cum, tok = self.echo.get(ws, (-1, None))
        if last_cum >= cum:
            return tok
        qeng = self.QSRC[ws]
        es = qeng + "_echo"
        self.count[es] += 1
        tok = (es, self.count[es])
        self.ops.append((qeng, lambda en: en.nop(), ((ws, cum),), es, 1))
        self.echo[ws] = (cum, tok)
        return tok

    def emit(self, eng, fn, reads=(), writes=(), dma=False, cc=False, extra_waits=()):
        waits = set()
        for b in reads:
            t = self.last_w.get(b)
            if t is not None:
                waits.add(t)
        for b in writes:
            t = self.last_w.get(b)
            if t is not None:
                waits.add(t)
            for t2 in self.readers.get(b, ()):
                waits.add(t2)
        for t in extra_waits:
            if t is not None:
                waits.add(t)
        stream = dma if isinstance(dma, str) else None
        fixed = set()
        for (ws, wv) in waits:
            if (ws.endswith("_dma") or ws == "cc") and self.QSRC.get(ws) != eng:
                fixed.add(self._echo(ws))
            else:
                fixed.add((ws, wv if not (ws.endswith("_dma") or ws == "cc")
                           else self.count[ws]))
        waits = fixed
        sem = "cc" if cc else (stream if stream else (eng + "_dma" if dma else eng))
        amt = 1 if cc else (16 if dma else 1)
        self.count[sem] += amt
        tok = (sem, self.count[sem])
        self.ops.append((eng, fn, tuple(waits), sem, amt))
        self.echo.pop(sem, None)
        for b in writes:
            self.last_w[b] = tok
            self.readers[b] = []
        for b in reads:
            self.readers.setdefault(b, []).append(tok)
        return tok

    def barrier(self):
        toks = []
        for e in self.ENGS:
            own = [(e, self.count[e]), (e + "_dma", self.count[e + "_dma"])]
            if e == "sync":
                own += [(s, self.count[s]) for s in ("x2t_dma", "hqt_dma", "w_dma")]
            if e == "gpsimd":
                own += [("wqin_dma", self.count["wqin_dma"])]
            toks.append(self.emit(e, lambda en: en.nop(), extra_waits=own))
        for e in self.ENGS:
            self.emit(e, lambda en: en.nop(), extra_waits=toks)

    def replay_new(self, block, sems):
        lo, hi = self.mark, len(self.ops)
        self.mark = hi
        sched = self

        def mk(eng_name):
            def go(eng):
                watermark = {}
                for idx in range(lo, hi):
                    e, fn, waits, sem, amt = sched.ops[idx]
                    if e != eng_name:
                        continue
                    for (ws, wv) in sorted(waits):
                        if watermark.get(ws, 0) >= wv:
                            continue
                        eng.wait_ge(sems[ws], wv)
                        watermark[ws] = wv
                    fn(eng).then_inc(sems[sem], amt)
            return go

        block.sync(mk("sync"))
        block.gpsimd(mk("gpsimd"))
        block.vector(mk("vector"))
        block.scalar(mk("scalar"))
        block.tensor(mk("tensor"))


def build_nc():
    nc = bass.Bass("TRN2", target_bir_lowering=False)

    dx = nc.dram_tensor("x", [T, D], F32, kind="ExternalInput").ap()
    dwq = nc.dram_tensor("wqt", [D, D], F32, kind="ExternalInput").ap()
    dwk = nc.dram_tensor("wkt", [D, D], F32, kind="ExternalInput").ap()
    dwv = nc.dram_tensor("wvt", [D, D], F32, kind="ExternalInput").ap()
    dwo = nc.dram_tensor("wot", [D, D], F32, kind="ExternalInput").ap()
    dbo = nc.dram_tensor("bo", [1, D], F32, kind="ExternalInput").ap()
    dpj = nc.dram_tensor("projp", [8, 128, M], F32, kind="ExternalInput").ap()
    dwg = nc.dram_tensor("wg_s", [IG, D], F32, kind="ExternalInput").ap()
    dwu = nc.dram_tensor("wu_s", [IG, D], F32, kind="ExternalInput").ap()
    dwd = nc.dram_tensor("wd_s", [DG, I], F32, kind="ExternalInput").ap()
    dout = nc.dram_tensor("out", [T, D], F32, kind="ExternalOutput").ap()

    NW_SH = IG * D * 2 + DG * I
    cc_w_in = nc.dram_tensor("ccw_in", [1, NW_SH], BF16)
    cc_w_out = nc.dram_tensor("ccw_out", [8, NW_SH], BF16)
    cc_kv_in = nc.dram_tensor("cckv_in", [H, 65, M], F32)
    cc_kv_out = nc.dram_tensor("cckv_out", [H, 65, M], F32)
    d_x1 = nc.dram_tensor("x1scr", [T, D], F32)

    dbg = {}
    def dump(name, ap_fn, shape, buf, dt=F32):
        if not DEBUG_DUMPS:
            return
        dbg[name] = nc.dram_tensor("dbg_" + name, shape, dt, kind="ExternalOutput")
        E("sync", lambda g, nm=name: g.dma_start(out=dbg[nm].ap(), in_=ap_fn()),
          reads=[buf], dma=True)

    es0 = ExitStack()
    sems = {}
    for s in Sched.ENGS + ("cc",):
        sems[s] = es0.enter_context(nc.semaphore("sem_" + s))
    for s in ("sync", "gpsimd", "scalar", "vector", "tensor"):
        sems[s + "_dma"] = es0.enter_context(nc.semaphore("sem_" + s + "d"))
        sems[s + "_echo"] = es0.enter_context(nc.semaphore("sem_" + s + "e"))
    for s in ("x2t_dma", "hqt_dma", "w_dma", "wqin_dma"):
        sems[s] = es0.enter_context(nc.semaphore("sem_" + s))
    S = Sched()
    E = S.emit

    def sb(stk, name, shape, dt=F32, side=None):
        return stk.enter_context(nc.sbuf_tensor(name, shape, dt, side=side))

    def psm(stk, name, shape, dt=F32):
        return stk.enter_context(nc.psum_tensor(name, shape, dt))

    pb = [psm(es0, "pb%d" % i, [128, 512], F32) for i in range(6)]
    pbf = psm(es0, "pbf", [128, 1024], BF16)
    pb7 = psm(es0, "pb7", [128, 512], F32)

    consts = sb(es0, "consts", [128, 4], F32)
    onescol = sb(es0, "onescol", [128, 1], BF16)
    ones_row = sb(es0, "ones_row", [1, 256], BF16)
    id128 = sb(es0, "id128", [128, 128], BF16)
    id128f = sb(es0, "id128f", [128, 128], F32)
    bo_row = sb(es0, "bo_row", [1, D], BF16)
    projp = sb(es0, "projp_s", [128, 8, M], BF16)
    onesblk = sb(es0, "onesblk", [128, 2], BF16)
    xs_ = [sb(es0, "xs%d" % i, [128, D], F32) for i in range(2)]
    sm = sb(es0, "sm", [128, 24], F32)

    def mkquant(stk, side=None, tag="", nsets=2):
        q = {"sets": [], "i": [0], "tag": tag, "pending": None}
        for si in range(nsets):
            s = {}
            for nm, shape, dt_ in (("m", [128, QCH // 16], F32),
                                   ("rc", [128, QCH // 16], F32),
                                   ("r", [128, QCH // 16], F32),
                                   ("s2", [128, QCH // 16], F32),
                                   ("ab", [128, QCH], F32),
                                   ("an", [128, QCH], F32),
                                   ("b", [128, QCH], F32),
                                   ("a", [128, QCH], F32),
                                   ("sg", [128, QCH], F32)):
                key = "q_%s%s%d" % (nm, tag, si)
                s[nm] = (sb(stk, key, shape, dt_, side=side), key)
            q["sets"].append(s)
        return q

    def emit_quant_stageB(q, hdl):
        s, Fc, P, xout, out_buf, post_fns = hdl
        NB = Fc // 16
        s2, s2N = s["s2"]; an, anN = s["an"]; b, bN = s["b"]; a, aN = s["a"]
        sg, sgN = s["sg"]
        for thr in (3.5, 2.5):
            E("vector", lambda v, th=thr: v.scalar_tensor_tensor(
                b[0:P, 0:Fc], an[0:P, 0:Fc], th, b[0:P, 0:Fc], OP.is_gt, OP.add),
              reads=[anN], writes=[bN])
        E("vector", lambda v: v.scalar_tensor_tensor(
            a[0:P, 0:Fc], b[0:P, 0:Fc], 2.0, a[0:P, 0:Fc], OP.mult, OP.add),
          reads=[bN], writes=[aN])
        E("vector", lambda v: v.tensor_tensor(
            a[0:P, 0:Fc].rearrange("p (b k) -> p b k", k=16),
            a[0:P, 0:Fc].rearrange("p (b k) -> p b k", k=16),
            s2[0:P, 0:NB].unsqueeze(2).broadcast_to([P, NB, 16]), OP.mult),
          reads=[s2N], writes=[aN])
        tok = E("vector", lambda v: v.tensor_tensor(
            xout, a[0:P, 0:Fc], sg[0:P, 0:Fc], OP.mult),
            reads=[aN, sgN], writes=[out_buf])
        for fn in post_fns:
            fn()
        return tok

    def flush_quant(q):
        hdl = q.get("pending")
        if hdl is not None:
            q["pending"] = None
            return emit_quant_stageB(q, hdl)
        return None

    def emit_quant_chunk(q, xin, xout, Fc, in_bufs, out_buf, P=128,
                         pipe=False, post_fns=()):
        NB = Fc // 16
        s = q["sets"][q["i"][0] % len(q["sets"])]
        q["i"][0] += 1
        m, mN = s["m"]; rc, rcN = s["rc"]; r, rN = s["r"]; s2, s2N = s["s2"]
        ab, abN = s["ab"]; an, anN = s["an"]; b, bN = s["b"]; a, aN = s["a"]
        sg, sgN = s["sg"]
        E("scalar", lambda ss: ss.activation(ab[0:P, 0:Fc], xin, AF.Abs, scale=6.0),
          reads=list(in_bufs), writes=[abN])
        E("scalar", lambda ss: ss.activation(sg[0:P, 0:Fc], xin, AF.Sign),
          reads=list(in_bufs), writes=[sgN])
        E("vector", lambda v: v.tensor_reduce(
            m[0:P, 0:NB], xin.rearrange("p (b k) -> p b k", k=16), AX.X, OP.max,
            apply_absolute_value=True), reads=list(in_bufs), writes=[mN])
        E("vector", lambda v: v.tensor_single_scalar(m[0:P, 0:NB], m[0:P, 0:NB], 6e-12, OP.max),
          writes=[mN])
        E("vector", lambda v: v.reciprocal(r[0:P, 0:NB], m[0:P, 0:NB]),
          reads=[mN], writes=[rN])
        E("vector", lambda v: v.tensor_single_scalar(s2[0:P, 0:NB], m[0:P, 0:NB], 1.0 / 12.0, OP.mult),
          reads=[mN], writes=[s2N])
        E("vector", lambda v: v.tensor_tensor(
            an[0:P, 0:Fc].rearrange("p (b k) -> p b k", k=16),
            ab[0:P, 0:Fc].rearrange("p (b k) -> p b k", k=16),
            r[0:P, 0:NB].unsqueeze(2).broadcast_to([P, NB, 16]), OP.mult),
          reads=[abN, rN], writes=[anN])
        E("gpsimd", lambda g: g.tensor_scalar(
            b[0:P, 0:Fc], an[0:P, 0:Fc], 5.0, 2.0, OP.is_gt, OP.mult),
          reads=[anN], writes=[bN])
        E("vector", lambda v: v.tensor_single_scalar(
            a[0:P, 0:Fc], an[0:P, 0:Fc], 0.25, OP.is_gt),
          reads=[anN], writes=[aN])
        for thr in (0.75, 1.25, 1.75):
            E("vector", lambda v, th=thr: v.scalar_tensor_tensor(
                a[0:P, 0:Fc], an[0:P, 0:Fc], th, a[0:P, 0:Fc], OP.is_gt, OP.add),
              reads=[anN], writes=[aN])
        hdl = (s, Fc, P, xout, out_buf, tuple(post_fns))
        if pipe:
            prev = q.get("pending")
            q["pending"] = hdl
            if prev is not None:
                return emit_quant_stageB(q, prev)
            return None
        return emit_quant_stageB(q, hdl)

    def emit_quant(q, xin_fn, xout_fn, Fcols, in_bufs, out_buf, pipe=False):
        c0 = 0
        while c0 < Fcols:
            fc = min(QCH, Fcols - c0)
            emit_quant_chunk(q, xin_fn(c0, fc), xout_fn(c0, fc), fc, in_bufs,
                             out_buf, pipe=pipe)
            c0 += fc

    def emit_ln(sqbuf, xbuf, xname, outap, outbuf):
        E("vector", lambda v: v.tensor_reduce(sm[:, 0:1], xbuf[:, :], AX.X, OP.add),
          reads=[xname], writes=["sm"])
        E("vector", lambda v: v.tensor_single_scalar(sm[:, 0:1], sm[:, 0:1], 1.0 / D, OP.mult),
          writes=["sm"])
        E("scalar", lambda s: s.activation(sqbuf[:, 0:D], xbuf[:, :], AF.Square,
                                           accum_out=sm[:, 4:5]),
          reads=[xname], writes=["sq_scr", "sm"])
        E("vector", lambda v: v.tensor_single_scalar(sm[:, 4:5], sm[:, 4:5], 1.0 / D, OP.mult),
          writes=["sm"])
        E("vector", lambda v: v.scalar_tensor_tensor(
            sm[:, 5:6], sm[:, 0:1], sm[:, 0:1], sm[:, 4:5], OP.mult, OP.subtract),
          writes=["sm"])
        E("scalar", lambda s: s.activation(sm[:, 6:7], sm[:, 5:6], AF.Sqrt,
                                           bias=consts[:, 0:1], scale=-1.0),
          reads=["consts"], writes=["sm"])
        E("vector", lambda v: v.reciprocal(sm[:, 7:8], sm[:, 6:7]), writes=["sm"])
        E("vector", lambda v: v.tensor_scalar(
            sm[:, 8:9], sm[:, 0:1], sm[:, 7:8], -1.0, OP.mult, OP.mult), writes=["sm"])
        return E("scalar", lambda s: s.activation(
            outap, xbuf[:, :], AF.Identity, bias=sm[:, 8:9], scale=sm[:, 7:8]),
            reads=[xname, "sm"], writes=[outbuf])

    block_cm = nc.Block()
    block = block_cm.__enter__()

    es1 = ExitStack()
    iota_s = sb(es1, "iota_s", [128, 128], F32, side="right")
    X1T = sb(es1, "X1T", [128, 8, T], BF16, side="right")
    w_aL = sb(es1, "w_aL", [128, 8, 512], BF16, side="right")
    w_aH = sb(es1, "w_aH", [128, 8, 512], BF16, side="right")
    w_kh = sb(es1, "w_kh", [128, 8, 512], BF16, side="right")
    xn_s = sb(es1, "xn_s", [128, 2, D], BF16, side="right")
    wq_in_ = [sb(es1, "wq_in%d" % i, [128, QCH], F32, side="right") for i in range(2)]
    wq_out_ = [sb(es1, "wq_out%d" % i, [128, QCH], BF16, side="right") for i in range(2)]
    wtr_ = [sb(es1, "wtr%d" % i, [128, QCH], BF16, side="right") for i in range(2)]
    sq1 = sb(es1, "sq1", [128, 2048], BF16, side="right")
    q1 = mkquant(es1, side="right")
    kfw = [sb(es1, "kfw%d" % i, [128, 2, T], BF16, side="right") for i in range(2)]
    kvst = sb(es1, "kvst", [65, 4, M], F32, side="right")

    es2 = ExitStack()
    Qf = sb(es2, "Qf", [128, 8, T], BF16)
    Vt = sb(es2, "Vt", [128, NT, H, 65], BF16)
    NMXQ = sb(es2, "NMXQ", [128, NT, H], F32)
    QSQ = sb(es2, "QSQ", [128, NT, H], F32)
    NMXK = sb(es2, "NMXK", [128, NT, H], F32)
    KSQ = sb(es2, "KSQ", [128, NT, H], F32)
    rawk = sb(es2, "rawk", [128, 4, M], BF16)
    vhat = sb(es2, "vhat", [128, 4, 65], BF16)
    vsum_r = sb(es2, "vsum_r", [1, H * 65], BF16)
    u_raw = sb(es2, "u_raw", [128, NT, H], F32)
    u_bf = sb(es2, "u_bf", [128, NT, H], BF16)
    urows = sb(es2, "urows", [16, NT, 128], BF16)

    E("vector", lambda v: v.memset(consts[:, 0:1], 1e-5), writes=["consts"])
    E("vector", lambda v: v.memset(consts[:, 1:2], -LN16), writes=["consts"])
    E("vector", lambda v: v.memset(onescol[:, :], 1.0), writes=["onescol"])
    E("vector", lambda v: v.memset(ones_row[:, :], 1.0), writes=["ones_row"])
    E("vector", lambda v: v.memset(onesblk[:, :], 0.0), writes=["onesblk"])
    E("vector", lambda v: v.memset(onesblk[0:64, 0:1], 1.0), writes=["onesblk"])
    E("vector", lambda v: v.memset(onesblk[64:128, 1:2], 1.0), writes=["onesblk"])
    E("gpsimd", lambda g: g.iota(iota_s[:, :], [[1, 128]], channel_multiplier=-1,
                                 allow_small_or_imprecise_dtypes=True), writes=["iota"])
    E("vector", lambda v: v.tensor_single_scalar(id128[:, :], iota_s[:, :], 0.0, OP.is_equal),
      reads=["iota"], writes=["id128"])
    E("vector", lambda v: v.tensor_single_scalar(id128f[:, :], iota_s[:, :], 0.0, OP.is_equal),
      reads=["iota"], writes=["id128f"])
    E("gpsimd", lambda g: g.dma_start(out=bo_row[:, :], in_=dbo), writes=["bo_row"], dma=True)
    E("gpsimd", lambda g: g.dma_start(out=projp[:, :, :], in_=dpj.rearrange("a p m -> p a m")),
      writes=["projp"], dma=True)
    E("gpsimd", lambda g: g.dma_start(
        out=w_aL[:, :, :], in_=dwq[:, 0:512].rearrange("(c p) e -> p c e", p=128)),
      writes=["w_aL"], dma=True)
    E("gpsimd", lambda g: g.dma_start(
        out=w_aH[:, :, :], in_=dwq[:, 512:1024].rearrange("(c p) e -> p c e", p=128)),
      writes=["w_aH"], dma=True)

    wq_tasks = []
    wbufi = [0]
    woff = 0
    secoff = 0
    for (dsrc, nrows, ncols) in ((dwg, IG, D), (dwu, IG, D), (dwd, DG, I)):
        secoff = {id(dwg): 0, id(dwu): IG * D, id(dwd): 2 * IG * D}[id(dsrc)]
        r0 = 0
        while r0 < nrows:
            rr = min(128, nrows - r0)
            c0 = 0
            while c0 < ncols:
                fc = min(QCH, ncols - c0)

                def wq_task(ds=dsrc, a=r0, b=rr, cc=c0, f=fc, sec=secoff,
                            nct=ncols // 128, il=nrows):
                    i = wbufi[0] % 2
                    wbufi[0] += 1
                    win, winN = wq_in_[i], "wq_in%d" % i
                    wout, woutN = wq_out_[i], "wq_out%d" % i
                    wtr, wtrN = wtr_[i], "wtr%d" % i
                    E("gpsimd", lambda g: g.dma_start(
                        out=win[0:b, 0:f], in_=ds[a:a + b, cc:cc + f]),
                      writes=[winN], dma="wqin_dma")
                    ncl = f // 128

                    def post():
                        for cl in range(ncl):
                            E("tensor", lambda te, cl_=cl: te.transpose(
                                pbf[0:128, cl_ * 128:cl_ * 128 + b],
                                wout[0:b, cl_ * 128:(cl_ + 1) * 128], id128[0:b, 0:b]),
                              reads=[woutN, "id128"], writes=["pbf"])
                        E("vector", lambda v: v.tensor_copy(
                            wtr[:, 0:ncl * b].rearrange("p (c i) -> p c i", c=ncl),
                            pbf[0:128, 0:ncl * 128].rearrange(
                                "p (c i) -> p c i", c=ncl)[:, :, 0:b]),
                          reads=["pbf"], writes=[wtrN])
                        dst = cc_w_in.ap()[0:1, sec:sec + 128 * nct * il].rearrange(
                            "o (p c i) -> (o p) c i", p=128, c=nct)[
                                :, cc // 128:cc // 128 + ncl, a:a + b]
                        E("gpsimd", lambda g, d_=dst: g.dma_start(
                            out=d_, in_=wtr[:, 0:ncl * b].rearrange(
                                "p (c i) -> p c i", c=ncl)),
                          reads=[wtrN], writes=["ccw"], dma=True)
                    emit_quant_chunk(q1, win[0:b, 0:f], wout[0:b, 0:f], f,
                                     [winN], woutN, P=b, pipe=True,
                                     post_fns=(post,))
                wq_tasks.append(wq_task)
                woff += rr * fc
                c0 += fc
            r0 += rr

    for t in range(NT):
        xbuf = xs_[t % 2]
        xname = "xs%d" % (t % 2)
        E("sync", lambda g, xb=xbuf, tt=t: g.dma_start(
            out=xb[:, :], in_=dx[tt * 128:(tt + 1) * 128, :]), writes=[xname], dma=True)
        xnb = xn_s[:, t % 2, :]
        xnn = "xn%d" % (t % 2)
        emit_ln(sq1, xbuf, xname, xnb, xnn)
        E("scalar", lambda s, ob=xnb, tt=t: s.dma_start_transpose(
            X1T[:, :, tt * 128:(tt + 1) * 128], ob),
          reads=[xnn], writes=["X1T"], dma=True)
        for _ in range(2):
            if wq_tasks:
                wq_tasks.pop(0)()
    while wq_tasks:
        wq_tasks.pop(0)()
    flush_quant(q1)

    for e in range(8):
        whalf, wname = (w_aL, "w_aL") if e < 4 else (w_aH, "w_aH")
        for g in range(4):
            bank = pb[(e * 4 + g) % 4]
            bname = "pb%d" % ((e * 4 + g) % 4)
            for c in range(8):
                E("tensor", lambda te, ee=e, gg=g, cc=c, wb=whalf, bk=bank: te.matmul(
                    bk[:, :], wb[:, cc, (ee % 4) * 128:(ee % 4 + 1) * 128],
                    X1T[:, cc, gg * 512:(gg + 1) * 512],
                    start=(cc == 0), stop=(cc == 7)),
                  reads=[wname, "X1T"] if c in (0, 7) else [],
                  writes=[bname] if c in (0, 7) else [])
            E("scalar", lambda s, ee=e, gg=g, bk=bank: s.activation(
                Qf[:, ee, gg * 512:(gg + 1) * 512], bk[:, :], AF.Identity, scale=0.125),
              reads=[bname], writes=["Qf"])

    E("gpsimd", lambda g: g.dma_start(
        out=w_aL[:, :, :], in_=dwv[:, 0:512].rearrange("(c p) e -> p c e", p=128)),
      writes=["w_aL"], dma=True)
    E("gpsimd", lambda g: g.dma_start(
        out=w_aH[:, :, :], in_=dwv[:, 512:1024].rearrange("(c p) e -> p c e", p=128)),
      writes=["w_aH"], dma=True)
    for t in range(NT):
        for g in range(2):
            whalf, wname = (w_aL, "w_aL") if g == 0 else (w_aH, "w_aH")
            bank = pb[(t * 2 + g) % 4]
            bname = "pb%d" % ((t * 2 + g) % 4)
            for c in range(8):
                E("tensor", lambda te, tt=t, cc=c, bk=bank, wb=whalf: te.matmul(
                    bk[:, :], X1T[:, cc, tt * 128:(tt + 1) * 128],
                    wb[:, cc, :],
                    start=(cc == 0), stop=(cc == 7)),
                  reads=[wname, "X1T"] if c in (0, 7) else [],
                  writes=[bname] if c in (0, 7) else [])
            E("vector", lambda v, tt=t, gg=g, bk=bank: v.tensor_copy(
                Vt[:, tt, gg * 8:(gg + 1) * 8, 0:64],
                bk[:, :].rearrange("p (h e) -> p h e", e=64)),
              reads=[bname], writes=["Vt"])
        E("vector", lambda v, tt=t: v.memset(Vt[:, tt, :, 64:65], 1.0), writes=["Vt"])

    dump("Qf", lambda: Qf[:, :, :], [128, 8, T], "Qf", BF16)
    dump("Vt", lambda: Vt[:, :, :, :], [128, NT, H, 65], "Vt", BF16)
    dump("X1T", lambda: X1T[:, :, :], [128, 8, T], "X1T", BF16)

    vw = ((pb[0], "pb0", 0, 512), (pb[1], "pb1", 512, 512), (pb[2], "pb2", 1024, 16))
    for (bank, bname, off, w) in vw:
        for t in range(NT):
            E("tensor", lambda te, tt=t, bk=bank, o=off, ww=w: te.matmul(
                bk[0:1, 0:ww], onescol[:, :],
                Vt[:, tt, :, :].rearrange("p h e -> p (h e)")[:, o:o + ww],
                start=(tt == 0), stop=(tt == NT - 1)),
              reads=["Vt", "onescol"] if t in (0, NT - 1) else [],
              writes=[bname] if t in (0, NT - 1) else [])
    E("vector", lambda v: v.tensor_single_scalar(vsum_r[0:1, 0:512], pb[0][0:1, :], E16, OP.mult),
      reads=["pb0"], writes=["vsum_r"])
    E("vector", lambda v: v.tensor_single_scalar(vsum_r[0:1, 512:1024], pb[1][0:1, :], E16, OP.mult),
      reads=["pb1"], writes=["vsum_r"])
    E("vector", lambda v: v.tensor_single_scalar(vsum_r[0:1, 1024:1040], pb[2][0:1, 0:16], E16, OP.mult),
      reads=["pb2"], writes=["vsum_r"])

    E("gpsimd", lambda g: g.dma_start(
        out=w_kh[:, :, :], in_=dwk[:, 0:512].rearrange("(c p) e -> p c e", p=128)),
      writes=["w_kh"], dma=True)
    for wave in range(4):
        heads = [wave * 4 + i for i in range(4)]
        kbuf = kfw[wave % 2]
        kname = "kfw%d" % (wave % 2)
        if wave == 2:
            E("gpsimd", lambda g: g.dma_start(
                out=w_kh[:, :, :], in_=dwk[:, 512:1024].rearrange("(c p) e -> p c e", p=128)),
              writes=["w_kh"], dma=True)
        for ei, e in enumerate((2 * wave, 2 * wave + 1)):
            for g in range(4):
                bank = pb[4 + (g % 2)]
                bname = "pb%d" % (4 + (g % 2))
                for c in range(8):
                    E("tensor", lambda te, ee=e, gg=g, cc=c, bk=bank: te.matmul(
                        bk[:, :], w_kh[:, cc, (ee % 4) * 128:(ee % 4 + 1) * 128],
                        X1T[:, cc, gg * 512:(gg + 1) * 512],
                        start=(cc == 0), stop=(cc == 7)),
                      reads=["w_kh", "X1T"] if c in (0, 7) else [],
                      writes=[bname] if c in (0, 7) else [])
                E("scalar", lambda s, ei_=ei, gg=g, bk=bank, kb_=kbuf: s.activation(
                    kb_[:, ei_, gg * 512:(gg + 1) * 512], bk[:, :], AF.Identity, scale=0.125),
                  reads=[bname], writes=[kname])
        for ei in range(2):
            E("scalar", lambda s, kb_=kbuf, ei_=ei: s.activation(
                sq1[:, :], kb_[:, ei_, :], AF.Square),
              reads=[kname], writes=["sq_scr"])
            cglob = 2 * wave + ei
            for t in range(NT):
                E("tensor", lambda te, tt=t, cg=cglob: te.matmul(
                    pb7[:, tt * 16 + 2 * cg: tt * 16 + 2 * cg + 2],
                    sq1[:, tt * 128:(tt + 1) * 128], onesblk[:, :],
                    start=True, stop=True),
                  reads=["sq_scr", "onesblk"] if t in (0, NT - 1) else [],
                  writes=["pb7"] if ((t == 0 and ei == 0) or (t == NT - 1 and ei == 1)) else [])
        E("vector", lambda v, w_=wave: v.tensor_copy(
            KSQ[:, :, 4 * w_:4 * w_ + 4],
            pb7[:, 0:256].rearrange("p (t h) -> p t h", h=16)[:, :, 4 * w_:4 * w_ + 4]),
          reads=["pb7"], writes=["KSQ"])
        XWB = (pb[4], pb[5], pb7)
        XWN = ("pb4", "pb5", "pb7")

        def kw_xw(k, t, sub):
            hh2 = heads[2 * sub:2 * sub + 2]
            for j, h in enumerate(hh2):
                i = 2 * sub + j
                bank = XWB[(2 * k + j) % 3]
                bname = XWN[(2 * k + j) % 3]
                hr = (h % 2) * 64
                ei = i // 2
                hp = 2 * wave + i // 2
                E("tensor", lambda te, bk=bank, kb_=kbuf, ei_=ei, hr_=hr, hp_=hp, tt=t: te.matmul(
                    bk[:, 0:256],
                    kb_[hr_:hr_ + 64, ei_, tt * 128:(tt + 1) * 128],
                    projp[hr_:hr_ + 64, hp_, :],
                    start=True, stop=True, tile_position=(hr_, 0)),
                  reads=[kname, "projp"], writes=[bname])

        def kw_tail(k, t, sub):
            hh2 = heads[2 * sub:2 * sub + 2]
            for j, h in enumerate(hh2):
                bname = XWN[(2 * k + j) % 3]
                E("vector", lambda v, bk=XWB[(2 * k + j) % 3], h_=h, tt=t: v.tensor_reduce(
                    NMXK[:, tt, h_:h_ + 1],
                    bk[:, 0:256], AX.X, OP.max, negate=True),
                  reads=[bname], writes=["NMXK"])
            E("vector", lambda v, h0=hh2[0], tt=t, sb=sub: v.scalar_tensor_tensor(
                sm[:, 12 + 2 * sb:14 + 2 * sb], KSQ[:, tt, h0:h0 + 2], -0.5,
                NMXK[:, tt, h0:h0 + 2], OP.mult, OP.add),
              reads=["KSQ", "NMXK"], writes=["smk%d" % sub])
            E("scalar", lambda s, sb=sub: s.activation(
                sm[:, 16 + 2 * sb:18 + 2 * sb], sm[:, 12 + 2 * sb:14 + 2 * sb],
                AF.Exp, bias=consts[:, 1:2]),
              reads=["smk%d" % sub, "consts"], writes=["smk2%d" % sub])
            for j, h in enumerate(hh2):
                i = 2 * sub + j
                bname = XWN[(2 * k + j) % 3]
                E("scalar", lambda s, bk=XWB[(2 * k + j) % 3], i_=i: s.activation(
                    rawk[:, i_, :], bk[:, 0:256], AF.Exp),
                  reads=[bname], writes=["rawk%d" % i])

        def kw_tail2(t, sub):
            hh2 = heads[2 * sub:2 * sub + 2]
            for j, h in enumerate(hh2):
                i = 2 * sub + j
                E("vector", lambda v, i_=i, h_=h, tt=t, sb=sub: v.tensor_scalar(
                    vhat[:, i_, :], Vt[:, tt, h_, :], sm[:, 16 + i_:17 + i_], 1.0,
                    OP.mult, OP.mult),
                  reads=["Vt", "smk2%d" % sub], writes=["vhat%d" % i])
            for j, h in enumerate(hh2):
                i = 2 * sub + j
                bank = pb[i]
                bname = "pb%d" % i
                E("tensor", lambda te, i_=i, bk=bank, tt=t: te.matmul(
                    bk[0:65, 0:256], vhat[:, i_, :], rawk[:, i_, :],
                    start=(tt == 0), stop=False),
                  reads=["vhat%d" % i, "rawk%d" % i],
                  writes=[bname] if t == 0 else [])

        items = [(t, sub) for t in range(NT) for sub in range(2)]
        kw_xw(0, *items[0])
        prev = None
        for k, it in enumerate(items):
            kw_tail(k, *it)
            if k + 1 < len(items):
                kw_xw(k + 1, *items[k + 1])
            if prev is not None:
                kw_tail2(*prev)
            prev = it
        kw_tail2(*prev)
        for i, h in enumerate(heads):
            bank = pb[i]
            bname = "pb%d" % i
            E("tensor", lambda te, bk=bank, h_=h: te.matmul(
                bk[0:65, 0:256], vsum_r[0:1, h_ * 65:(h_ + 1) * 65],
                ones_row[0:1, :], start=False, stop=True),
              reads=["vsum_r", "ones_row"], writes=[bname])
            E("vector", lambda v, bk=bank, i_=i: v.tensor_copy(
                kvst[:, i_, :], bk[0:65, 0:256]),
              reads=[bname], writes=["kvst"])
        E("gpsimd", lambda g, w_=wave: g.dma_start(
            out=cc_kv_in.ap()[4 * w_:4 * w_ + 4, :, :].rearrange("h d m -> d h m"),
            in_=kvst[:, :, :]),
          reads=["kvst"], writes=["cckv"], dma=True)

    dump("KSQ", lambda: KSQ[:, :, :], [128, NT, H], "KSQ")
    dump("NMXK", lambda: NMXK[:, :, :], [128, NT, H], "NMXK")
    dump("kvin", lambda: cc_kv_in.ap(), [H, 65, M], "cckv")
    E("gpsimd", lambda g: g.collective_compute(
        "AllReduce", OP.add, replica_groups=[[0, 1, 2, 3], [4, 5, 6, 7]],
        ins=[cc_kv_in.ap().opt()], outs=[cc_kv_out.ap().opt()]),
      reads=["cckv"], writes=["cckv_out"], cc=True)

    for c in range(8):
        E("scalar", lambda s, cc=c: s.activation(sq1[:, :], Qf[:, cc, :], AF.Square),
          reads=["Qf"], writes=["sq_scr"])
        for t in range(NT):
            E("tensor", lambda te, tt=t, cc=c: te.matmul(
                pb7[:, tt * 16 + 2 * cc: tt * 16 + 2 * cc + 2],
                sq1[:, tt * 128:(tt + 1) * 128], onesblk[:, :],
                start=True, stop=True),
              reads=["sq_scr", "onesblk"] if t in (0, NT - 1) else [],
              writes=["pb7"] if ((t == 0 and c == 0) or (t == NT - 1 and c == 7)) else [])
    E("vector", lambda v: v.tensor_copy(
        QSQ[:, :, :].rearrange("p t h -> p (t h)"), pb7[:, 0:256]),
      reads=["pb7"], writes=["QSQ"])

    for wave in range(4):
        heads = [wave * 4 + i for i in range(4)]
        XWB = (0, 1, 4, 5)
        for t in range(NT):
            for i, h in enumerate(heads):
                bank = pb[XWB[i]]
                bname = "pb%d" % XWB[i]
                hp, hr = h // 2, (h % 2) * 64
                E("tensor", lambda te, bk=bank, hp_=hp, hr_=hr, tt=t: te.matmul(
                    bk[:, 0:256],
                    Qf[hr_:hr_ + 64, hp_, tt * 128:(tt + 1) * 128],
                    projp[hr_:hr_ + 64, hp_, :],
                    start=True, stop=True, tile_position=(hr_, 0)),
                  reads=["Qf", "projp"], writes=[bname])
            for i, h in enumerate(heads):
                bname = "pb%d" % XWB[i]
                E("vector", lambda v, bk=pb[XWB[i]], h_=h, tt=t: v.tensor_reduce(
                    NMXQ[:, tt, h_:h_ + 1],
                    bk[:, 0:256], AX.X, OP.max, negate=True),
                  reads=[bname], writes=["NMXQ"])

    E("vector", lambda v: v.scalar_tensor_tensor(
        u_raw[:, :, :].rearrange("p t h -> p (t h)"),
        QSQ[:, :, :].rearrange("p t h -> p (t h)"), 0.5,
        NMXQ[:, :, :].rearrange("p t h -> p (t h)"), OP.mult, OP.subtract),
      reads=["QSQ", "NMXQ"], writes=["u_raw"])
    E("scalar", lambda s: s.activation(
        u_bf[:, :, :].rearrange("p t h -> p (t h)"),
        u_raw[:, :, :].rearrange("p t h -> p (t h)"), AF.Exp),
      reads=["u_raw"], writes=["u_bf"])
    E("scalar", lambda s: s.activation(
        u_raw[:, :, :].rearrange("p t h -> p (t h)"),
        u_raw[:, :, :].rearrange("p t h -> p (t h)"), AF.Exp),
      reads=["u_raw"], writes=["u_raw"])
    for t in range(NT):
        E("tensor", lambda te, tt=t: te.transpose(
            pbf[0:16, 0:128], u_bf[:, tt, :], id128[:, :]),
          reads=["u_bf", "id128"], writes=["pbf"])
        E("vector", lambda v, tt=t: v.tensor_copy(urows[:, tt, :], pbf[0:16, 0:128]),
          reads=["pbf"], writes=["urows"])

    dump("NMXQ", lambda: NMXQ[:, :, :], [128, NT, H], "NMXQ")
    dump("u_raw", lambda: u_raw[:, :, :], [128, NT, H], "u_raw")
    dump("u_bf", lambda: u_bf[:, :, :], [128, NT, H], "u_bf", BF16)
    dump("urows", lambda: urows[:, :, :], [16, NT, 128], "urows", BF16)
    S.barrier()
    S.replay_new(block, sems)
    es1.close()

    es3 = ExitStack()
    w_b = sb(es3, "w_b", [128, 8, D], BF16, side="right")
    kvb = sb(es3, "kvb", [65, H, M], BF16, side="right")
    Ccol = sb(es3, "Ccol", [65, H], F32, side="right")
    ccol_bf = sb(es3, "ccol_bf", [65, H], BF16, side="right")
    ct_sb = sb(es3, "ct_sb", [16, 65], BF16, side="right")
    cflat = sb(es3, "cflat", [1, 16 * 65], BF16, side="right")
    kvt = sb(es3, "kvt", [128, 2, H, 65], BF16, side="right")
    dstg = [sb(es3, "dstg0", [1, 512], F32, side="right")]
    rawq = sb(es3, "rawq", [128, 2, 2, T], BF16, side="right")
    Mg = sb(es3, "Mg", [128, 8, T], BF16, side="right")
    denF = sb(es3, "denF", [16, T], F32, side="right")
    denT = sb(es3, "denT", [128, NT, H], F32, side="right")
    r_bf = sb(es3, "r_bf", [128, NT, H], BF16, side="right")
    rrows = sb(es3, "rrows", [16, NT, 128], BF16, side="right")
    uflat2 = sb(es3, "uflat2", [1, 2 * T], BF16, side="right")
    rflat2 = None
    srep = sb(es3, "srep", [128, H], F32, side="right")
    rbc = sb(es3, "rbc", [128, 512], F32, side="right")

    E("gpsimd", lambda g: g.dma_start(
        out=w_b[:, :, :], in_=dwo.rearrange("(c p) e -> p c e", p=128)),
      writes=["w_b"], dma=True)
    E("gpsimd", lambda g: g.dma_start(
        out=kvb[:, :, :], in_=cc_kv_out.ap().rearrange("h d m -> d h m")),
      reads=["cckv_out"], writes=["kvb"], dma=True)
    E("vector", lambda v: v.nop(), reads=["kvb", "w_b"])
    E("gpsimd", lambda g: g.collective_compute(
        "AllGather", OP.bypass, replica_groups=[list(range(8))],
        ins=[cc_w_in.ap().opt()], outs=[cc_w_out.ap().opt()]),
      reads=["ccw"], writes=["ccw_out"], cc=True)
    E("vector", lambda v: v.tensor_reduce(Ccol[:, :], kvb[:, :, :], AX.X, OP.add),
      reads=["kvb"], writes=["Ccol"])
    E("vector", lambda v: v.tensor_single_scalar(Ccol[64:65, :], Ccol[64:65, :], 16.0, OP.add),
      writes=["Ccol"])
    E("vector", lambda v: v.tensor_single_scalar(Ccol[:, :], Ccol[:, :], 16.0 * E16, OP.mult),
      writes=["Ccol"])
    E("vector", lambda v: v.tensor_copy(ccol_bf[:, :], Ccol[:, :]),
      reads=["Ccol"], writes=["ccol_bf"])
    E("tensor", lambda te: te.transpose(pbf[0:16, 0:65], ccol_bf[:, :], id128[0:65, 0:65]),
      reads=["ccol_bf", "id128"], writes=["pbf"])
    E("vector", lambda v: v.tensor_copy(ct_sb[:, :], pbf[0:16, 0:65]),
      reads=["pbf"], writes=["ct_sb"])
    E("scalar", lambda s: s.dma_start(
        out=cflat[0:1, :].rearrange("o (h e) -> o h e", h=16), in_=ct_sb[:, :]),
      reads=["ct_sb", "trq"], writes=["cflat", "trq"], dma=True)
    for h in range(H):
        for c in range(2):
            E("tensor", lambda te, h_=h, c_=c: te.transpose(
                pbf[0:128, 0:65], kvb[:, h_, c_ * 128:(c_ + 1) * 128], id128[0:65, 0:65]),
              reads=["kvb", "id128"], writes=["pbf"])
            E("vector", lambda v, h_=h, c_=c: v.tensor_copy(kvt[:, c_, h_, :], pbf[0:128, 0:65]),
              reads=["pbf"], writes=["kvt"])
    E("tensor", lambda te: te.matmul(
        pb7[:, 0:16], ones_row[0:1, 0:128],
        cflat[0:1, :].rearrange("o (h e) -> o h e", e=65)[:, :, 64:65]
            .rearrange("o h z -> o (h z)"),
        start=True, stop=True),
      reads=["cflat", "ones_row"], writes=["pb7"])
    E("vector", lambda v: v.tensor_copy(srep[:, :], pb7[:, 0:16]),
      reads=["pb7"], writes=["srep"])

    for p in range(8):
        hA, hB = 2 * p, 2 * p + 1
        E("scalar", lambda s, pp=p: s.dma_start(
            out=uflat2[0:1, :].rearrange("o (i t k) -> o i t k", i=2, t=NT),
            in_=urows[2 * pp:2 * pp + 2, :, :]),
          reads=["urows", "trq"], writes=["uflat2", "trq"], dma=True)
        for k, (c, i, g) in enumerate(
                [(c, i, g) for c in range(2) for i in range(2) for g in range(4)]):
            hr = (0, 64)[i]
            bank = (pb[0], pb[1], pb7)[k % 3]
            bname = ("pb0", "pb1", "pb7")[k % 3]
            E("tensor", lambda te, bk=bank, c_=c, hr_=hr, pp=p, gg=g: te.matmul(
                bk[:, 0:512],
                projp[hr_:hr_ + 64, pp, c_ * 128:(c_ + 1) * 128],
                Qf[hr_:hr_ + 64, pp, gg * 512:(gg + 1) * 512],
                start=True, stop=True, tile_position=(hr_, 0)),
              reads=["projp", "Qf"], writes=[bname])
            E("scalar", lambda s, bk=bank, c_=c, i_=i, gg=g: s.activation(
                rawq[:, c_, i_, gg * 512:(gg + 1) * 512], bk[:, 0:512], AF.Exp),
              reads=[bname], writes=["rawq"])
        for g in range(4):
            bank = pb[2 + g]
            bname = "pb%d" % (2 + g)
            for i in range(2):
                h_ = (hA, hB)[i]
                E("tensor", lambda te, bk=bank, i_=i, h__=h_, gg=g: te.matmul(
                    bk[0:65, :],
                    kvt[:, 0, h__, 0:65],
                    rawq[:, 0, i_, gg * 512:(gg + 1) * 512],
                    start=True, stop=False),
                  reads=["kvt", "rawq"], writes=[bname])
                E("tensor", lambda te, bk=bank, i_=i, h__=h_, gg=g: te.matmul(
                    bk[0:64, :],
                    cflat[0:1, h__ * 65:h__ * 65 + 64],
                    uflat2[0:1, i_ * T + gg * 512:i_ * T + (gg + 1) * 512],
                    start=False, stop=False),
                  reads=["cflat", "uflat2"])
                E("tensor", lambda te, bk=bank, i_=i, h__=h_, gg=g: te.matmul(
                    bk[0:65, :],
                    kvt[:, 1, h__, 0:65],
                    rawq[:, 1, i_, gg * 512:(gg + 1) * 512],
                    start=False, stop=True),
                  reads=["kvt", "rawq"], writes=[bname])
                E("vector", lambda v, bk=bank, pp=p, i_=i, gg=g: v.tensor_copy(
                    Mg[64 * i_:64 * i_ + 64, pp, gg * 512:(gg + 1) * 512],
                    bk[0:64, :]),
                  reads=[bname], writes=["Mg"])
                ds = dstg[0]
                dsn = "dstg0"
                E("vector", lambda v, bk=bank, d_=ds: v.tensor_copy(
                    d_[0:1, :], bk[64:65, :]),
                  reads=[bname], writes=[dsn])
                E("sync", lambda gq, d_=ds, h__=h_, gg=g: gq.dma_start(
                    out=denF[h__:h__ + 1, gg * 512:(gg + 1) * 512], in_=d_[0:1, :]),
                  reads=[dsn], writes=["denF"], dma=True)

    dump("kvb", lambda: kvb[:, :, :], [65, H, M], "kvb", BF16)
    dump("cflat", lambda: cflat[0:1, :], [1, 16 * 65], "cflat", BF16)
    dump("kvt", lambda: kvt[:, :, :, :], [128, 2, H, 65], "kvt", BF16)
    dump("MgN", lambda: Mg[:, :, :], [128, 8, T], "Mg", BF16)

    for t in range(NT):
        E("tensor", lambda te, tt=t: te.transpose(
            pb7[0:128, 0:16], denF[:, tt * 128:(tt + 1) * 128], id128f[0:16, 0:16]),
          reads=["denF", "id128f"], writes=["pb7"])
        E("vector", lambda v, tt=t: v.tensor_copy(denT[:, tt, :], pb7[0:128, 0:16]),
          reads=["pb7"], writes=["denT"])
    for t in range(NT):
        E("vector", lambda v, tt=t: v.tensor_tensor(
            sm[:, 0:16], u_raw[:, tt, :], srep[:, :], OP.mult),
          reads=["u_raw", "srep"], writes=["sm"])
        E("vector", lambda v, tt=t: v.tensor_tensor(
            denT[:, tt, :], denT[:, tt, :], sm[:, 0:16], OP.add),
          reads=["sm"], writes=["denT"])
    E("vector", lambda v: v.reciprocal(
        denT[:, :, :].rearrange("p t h -> p (t h)"),
        denT[:, :, :].rearrange("p t h -> p (t h)")), writes=["denT"])
    E("vector", lambda v: v.tensor_copy(
        r_bf[:, :, :].rearrange("p t h -> p (t h)"),
        denT[:, :, :].rearrange("p t h -> p (t h)")),
      reads=["denT"], writes=["r_bf"])
    for t in range(NT):
        E("tensor", lambda te, tt=t: te.transpose(
            pbf[0:16, 0:128], r_bf[:, tt, :], id128[:, :]),
          reads=["r_bf", "id128"], writes=["pbf"])
        E("vector", lambda v, tt=t: v.tensor_copy(rrows[:, tt, :], pbf[0:16, 0:128]),
          reads=["pbf"], writes=["rrows"])

    dump("denF", lambda: denF[:, :], [16, T], "denF", F32)
    dump("denT", lambda: denT[:, :, :], [128, NT, H], "denT")

    for p in range(8):
        hA, hB = 2 * p, 2 * p + 1
        E("scalar", lambda s, pp=p: s.dma_start(
            out=uflat2[0:1, :].rearrange("o (i t k) -> o i t k", i=2, t=NT),
            in_=rrows[2 * pp:2 * pp + 2, :, :]),
          reads=["rrows", "trq"], writes=["uflat2", "trq"], dma=True)
        for g in range(4):
            for i, pbase in enumerate((0, 64)):
                E("tensor", lambda te, pb_=pbase, i_=i, gg=g: te.matmul(
                    pb7[pb_:pb_ + 64, :],
                    ones_row[0:1, 0:64],
                    uflat2[0:1, i_ * T + gg * 512:i_ * T + (gg + 1) * 512],
                    start=True, stop=True, tile_position=(0, pb_)),
                  reads=["uflat2", "ones_row"], writes=["pb7"])
            E("scalar", lambda s: s.copy(rbc[:, :], pb7[:, :]),
              reads=["pb7"], writes=["rbc"])
            E("vector", lambda v, pp=p, gg=g: v.tensor_tensor(
                Mg[:, pp, gg * 512:(gg + 1) * 512],
                Mg[:, pp, gg * 512:(gg + 1) * 512],
                rbc[:, :], OP.mult),
              reads=["rbc"], writes=["Mg"])

    dump("Mg", lambda: Mg[:, :, :], [128, 8, T], "Mg", BF16)

    for t in range(NT):
        xbuf = xs_[t % 2]
        xname = "xs%d" % (t % 2)
        E("sync", lambda g, xb=xbuf, tt=t: g.dma_start(
            out=xb[:, :], in_=dx[tt * 128:(tt + 1) * 128, :]), writes=[xname], dma=True)
        for g in range(2):
            bank = pb[g]
            bname = "pb%d" % g
            for c in range(8):
                E("tensor", lambda te, bk=bank, cc=c, tt=t, gg=g: te.matmul(
                    bk[:, :], Mg[:, cc, tt * 128:(tt + 1) * 128],
                    w_b[:, cc, gg * 512:(gg + 1) * 512],
                    start=(cc == 0), stop=False),
                  reads=["Mg", "w_b"] if c in (0, 7) else [],
                  writes=[bname] if c == 0 else [])
            E("tensor", lambda te, bk=bank, gg=g: te.matmul(
                bk[:, :], ones_row[0:1, 0:128], bo_row[0:1, gg * 512:(gg + 1) * 512],
                start=False, stop=True),
              reads=["bo_row", "ones_row"], writes=[bname])
            E("vector", lambda v, xb=xbuf, bk=bank, gg=g: v.tensor_tensor(
                xb[:, gg * 512:(gg + 1) * 512], xb[:, gg * 512:(gg + 1) * 512],
                bk[:, :], OP.add),
              reads=[bname, xname], writes=[xname])
        E("sync", lambda g, xb=xbuf, tt=t: g.dma_start(
            out=d_x1.ap()[tt * 128:(tt + 1) * 128, :], in_=xb[:, :]),
          reads=[xname], writes=["x1_%d" % t], dma=True)

    dump("x1", lambda: d_x1.ap(), [T, D], "x1_%d" % (NT - 1))
    S.barrier()
    S.replay_new(block, sems)
    es2.close()
    es3.close()

    es4 = ExitStack()
    wgT = sb(es4, "wgT", [128, 8, I], BF16)
    wuT = sb(es4, "wuT", [128, 8, I], BF16)
    wdT = sb(es4, "wdT", [128, 22, D], BF16)
    x2t_ = [sb(es4, "x2t%d" % i, [128, 8, 128], BF16) for i in range(2)]
    xq2_ = [sb(es4, "xq2%d" % i, [128, D], BF16) for i in range(2)]
    xn2_ = [sb(es4, "xn2%d" % i, [128, D], F32) for i in range(2)]
    hch_ = [sb(es4, "hch%d" % i, [128, 512], F32) for i in range(2)]
    hqc_ = [sb(es4, "hqc%d" % i, [128, 512], BF16) for i in range(3)]
    hqT_ = [sb(es4, "hqT%d" % i, [128, 4, 128], BF16) for i in range(3)]
    gs_ = [sb(es4, "gs%d" % i, [128, 512], F32) for i in range(2)]
    sq3 = sb(es4, "sq3", [128, D], BF16)
    q3 = mkquant(es4, tag="3")

    for r in range(8):
        for (dstT, dname, nrows, ncols, sec, qeng) in (
                (wgT, "wgT", IG, D, 0, "sync"),
                (wuT, "wuT", IG, D, IG * D, "gpsimd"),
                (wdT, "wdT", DG, I, 2 * IG * D, "sync")):
            nct = ncols // 128
            src = cc_w_out.ap()[r:r + 1, sec:sec + 128 * nct * nrows].rearrange(
                "o (p c i) -> (o p) c i", p=128, c=nct)
            E(qeng, lambda s, sr=src, dT=dstT, a0=r * nrows, a1=(r + 1) * nrows:
                s.dma_start(out=dT[:, :, a0:a1], in_=sr),
              reads=["ccw_out"], writes=[dname],
              dma=("w_dma" if qeng == "sync" else True))
    dump("wgTd", lambda: wgT[:, :, :], [128, 8, I], "wgT", BF16)
    dump("wuTd", lambda: wuT[:, :, :], [128, 8, I], "wuT", BF16)
    dump("wdTd", lambda: wdT[:, :, :], [128, 22, D], "wdT", BF16)

    NGRP = 6

    def ffn_down_chunk(t, g):
        wl = 512 if g < 5 else 256
        hqT = hqT_[g % 3]
        hqTn = "hqT%d" % (g % 3)
        ncl = wl // 128
        for cl in range(ncl):
            cc = g * 4 + cl
            for half in range(2):
                bank = pb[4 + half]
                bname = "pb%d" % (4 + half)
                first = (g == 0 and cl == 0)
                last = (g == NGRP - 1 and cl == ncl - 1)
                E("tensor", lambda te, bk=bank, hT=hqT, cl_=cl, cc_=cc, hf=half,
                    f=first, l=last: te.matmul(
                    bk[:, :], hT[:, cl_, :], wdT[:, cc_, hf * 512:(hf + 1) * 512],
                    start=f, stop=l),
                  reads=[hqTn, "wdT"],
                  writes=[bname] if (first or last) else [])

    def ffn_residual_out(t):
        xbuf = xs_[t % 2]
        xname = "xs%d" % (t % 2)
        for half in range(2):
            bank = pb[4 + half]
            bname = "pb%d" % (4 + half)
            E("vector", lambda v, xb=xbuf, bk=bank, hf=half: v.tensor_tensor(
                xb[:, hf * 512:(hf + 1) * 512], xb[:, hf * 512:(hf + 1) * 512],
                bk[:, :], OP.add),
              reads=[bname, xname], writes=[xname])
        E("gpsimd", lambda g, xb=xbuf, tt=t: g.dma_start(
            out=dout[tt * 128:(tt + 1) * 128, :], in_=xb[:, :]),
          reads=[xname], writes=["out_%d" % t], dma=True)

    def ffn_front(t):
        xbuf = xs_[t % 2]
        xname = "xs%d" % (t % 2)
        xn2 = xn2_[t % 2]
        xn2n = "xn2%d" % (t % 2)
        xq2 = xq2_[t % 2]
        xq2n = "xq2%d" % (t % 2)
        x2t = x2t_[t % 2]
        x2tn = "x2t%d" % (t % 2)
        E("gpsimd", lambda g, xb=xbuf, tt=t: g.dma_start(
            out=xb[:, :], in_=d_x1.ap()[tt * 128:(tt + 1) * 128, :]),
          reads=["x1_%d" % t], writes=[xname], dma=True)
        emit_ln(sq3, xbuf, xname, xn2[:, :], xn2n)
        emit_quant(q3, lambda c0, fc: xn2[:, c0:c0 + fc],
                   lambda c0, fc: xq2[:, c0:c0 + fc], D, [xn2n], xq2n, pipe=True)
        flush_quant(q3)
        E("sync", lambda s, xt=x2t, xq=xq2: s.dma_start_transpose(
            xt[:, :, :], xq[:, :]),
          reads=[xq2n], writes=[x2tn], dma="x2t_dma")

    for t in range(NT):
        x2t = x2t_[t % 2]
        x2tn = "x2t%d" % (t % 2)
        ffn_front(t)
        if t >= 1:
            ffn_residual_out(t - 1)
        for g in range(NGRP):
            w0 = g * 512
            wlen = 512 if g < 5 else 256
            bkg = pb[(2 * g) % 4]
            bng = "pb%d" % ((2 * g) % 4)
            bku = pb[(2 * g + 1) % 4]
            bnu = "pb%d" % ((2 * g + 1) % 4)
            for c in range(8):
                E("tensor", lambda te, bk=bkg, xt=x2t, cc=c, w0_=w0, wl=wlen: te.matmul(
                    bk[:, 0:wl], xt[:, cc, :], wgT[:, cc, w0_:w0_ + wl],
                    start=(cc == 0), stop=(cc == 7)),
                  reads=[x2tn, "wgT"] if c in (0, 7) else [],
                  writes=[bng] if c in (0, 7) else [])
            for c in range(8):
                E("tensor", lambda te, bk=bku, xt=x2t, cc=c, w0_=w0, wl=wlen: te.matmul(
                    bk[:, 0:wl], xt[:, cc, :], wuT[:, cc, w0_:w0_ + wl],
                    start=(cc == 0), stop=(cc == 7)),
                  reads=[x2tn, "wuT"] if c in (0, 7) else [],
                  writes=[bnu] if c in (0, 7) else [])
            if g >= 2:
                ffn_down_chunk(t, g - 2)
            gsb, gsn = gs_[g % 2], "gs%d" % (g % 2)
            hch, hchn = hch_[g % 2], "hch%d" % (g % 2)
            hqc, hqcn = hqc_[g % 3], "hqc%d" % (g % 3)
            hqT, hqTn = hqT_[g % 3], "hqT%d" % (g % 3)
            E("scalar", lambda s, bk=bkg, gb=gsb, wl=wlen: s.activation(
                gb[:, 0:wl], bk[:, 0:wl], AF.Silu),
              reads=[bng], writes=[gsn])
            E("vector", lambda v, bk=bku, hb=hch, gb=gsb, wl=wlen: v.tensor_tensor(
                hb[:, 0:wl], bk[:, 0:wl], gb[:, 0:wl], OP.mult),
              reads=[bnu, gsn], writes=[hchn])
            emit_quant_chunk(
                q3, hch[0:128, 0:wlen], hqc[0:128, 0:wlen], wlen,
                [hchn], hqcn, pipe=True,
                post_fns=(lambda hT=hqT, hb=hqc, wl=wlen, hqcn_=hqcn,
                          hqTn_=hqTn: E(
                              "sync", lambda s: s.dma_start_transpose(
                                  hT[:, 0:wl // 128, :], hb[0:128, 0:wl]),
                              reads=[hqcn_], writes=[hqTn_], dma="hqt_dma"),))
        ffn_down_chunk(t, NGRP - 2)
        flush_quant(q3)
        ffn_down_chunk(t, NGRP - 1)

    ffn_residual_out(NT - 1)
    dump("xn20d", lambda: xn2_[0][:, :], [128, D], "xn20")
    dump("xq20d", lambda: xq2_[0][:, :], [128, D], "xq20", BF16)
    dump("x2t0d", lambda: x2t_[0][:, :, :], [128, 8, 128], "x2t0", BF16)
    dump("hqT0d", lambda: hqT_[0][:, :, :], [128, 4, 128], "hqT0", BF16)
    dump("hqT1d", lambda: hqT_[1][:, :, :], [128, 4, 128], "hqT1", BF16)
    E("gpsimd", lambda g: g.nop(), reads=["out_%d" % (NT - 1)],
      extra_waits=[("gpsimd_dma", S.count["gpsimd_dma"])])
    S.replay_new(block, sems)
    es4.close()

    block_cm.__exit__(None, None, None)
    es0.close()
    return nc


_NC_CACHE = {}


def kernel(L0_attn_x, L0_attn_Wq, L0_attn_Wk, L0_attn_Wv, L0_attn_Wo, L0_attn_bo,
           L0_attn_proj, L0_ffn_w_gate, L0_ffn_w_up, L0_ffn_w_down):
    x = np.ascontiguousarray(np.asarray(L0_attn_x, dtype=np.float32))
    wqt = np.ascontiguousarray(np.asarray(L0_attn_Wq, np.float32).T)
    wkt = np.ascontiguousarray(np.asarray(L0_attn_Wk, np.float32).T)
    wvt = np.ascontiguousarray(np.asarray(L0_attn_Wv, np.float32).T)
    wot = np.ascontiguousarray(np.asarray(L0_attn_Wo, np.float32).T)
    bo = np.ascontiguousarray(np.asarray(L0_attn_bo, np.float32).reshape(1, D))
    pj = np.asarray(L0_attn_proj, np.float32).transpose(0, 2, 1)
    projp = np.ascontiguousarray(pj.reshape(8, 128, M))
    wg = np.asarray(L0_ffn_w_gate, np.float32)
    wu = np.asarray(L0_ffn_w_up, np.float32)
    wd = np.asarray(L0_ffn_w_down, np.float32)

    if "nc" not in _NC_CACHE:
        _NC_CACHE["nc"] = build_nc()
    nc = _NC_CACHE["nc"]

    in_maps = []
    for c in range(NCORE):
        b, s = c // 4, c % 4
        in_maps.append({
            "x": np.ascontiguousarray(x[b, s * T:(s + 1) * T, :]),
            "wqt": wqt, "wkt": wkt, "wvt": wvt, "wot": wot,
            "bo": bo, "projp": projp,
            "wg_s": np.ascontiguousarray(wg[c * IG:(c + 1) * IG, :]),
            "wu_s": np.ascontiguousarray(wu[c * IG:(c + 1) * IG, :]),
            "wd_s": np.ascontiguousarray(wd[c * DG:(c + 1) * DG, :]),
        })
    res = run_bass_kernel_spmd(nc, in_maps, core_ids=list(range(NCORE)))
    out = np.empty((B, TFULL, D), np.float32)
    for c in range(NCORE):
        b, s = c // 4, c % 4
        out[b, s * T:(s + 1) * T, :] = res.results[c]["out"]
    return out


# version 34 (final)
# speedup vs baseline: 1.3217x; 1.2468x over previous
import math
import numpy as np
import ml_dtypes
from contextlib import ExitStack

import concourse.bass as bass
import concourse.mybir as mybir
from concourse.bass_utils import run_bass_kernel_spmd
from concourse.alu_op_type import AluOpType as OP

F32 = mybir.dt.float32
BF16 = mybir.dt.bfloat16
BF = ml_dtypes.bfloat16
AF = mybir.ActivationFunctionType
AX = mybir.AxisListType

B, TFULL, D = 2, 8192, 1024
NCORE = 8
T = 2048
NT = T // 128
H, M, DK = 16, 256, 64
I = 2816
IG = I // 8
DG = D // 8
EPS = 1e-6
E16 = EPS / 16.0
LN16 = math.log(16.0)
QCH = 512
DEBUG_DUMPS = False


class Sched:
    ENGS = ("sync", "gpsimd", "vector", "scalar", "tensor")

    def __init__(self):
        self.ops = []
        self.mark = 0
        self.count = {e: 0 for e in self.ENGS}
        for e in self.ENGS:
            self.count[e + "_dma"] = 0
            self.count[e + "_echo"] = 0
        self.count["cc"] = 0
        self.count["x2t_dma"] = 0
        self.count["hqt_dma"] = 0
        self.count["w_dma"] = 0
        self.count["wqin_dma"] = 0
        self.last_w = {}
        self.readers = {}
        self.echo = {}

    QSRC = {"sync_dma": "sync", "gpsimd_dma": "gpsimd", "scalar_dma": "scalar",
            "vector_dma": "vector", "tensor_dma": "tensor", "cc": "gpsimd",
            "x2t_dma": "sync", "hqt_dma": "sync", "w_dma": "sync",
            "wqin_dma": "gpsimd"}

    def _echo(self, ws):
        cum = self.count[ws]
        last_cum, tok = self.echo.get(ws, (-1, None))
        if last_cum >= cum:
            return tok
        qeng = self.QSRC[ws]
        es = qeng + "_echo"
        self.count[es] += 1
        tok = (es, self.count[es])
        self.ops.append((qeng, lambda en: en.nop(), ((ws, cum),), es, 1))
        self.echo[ws] = (cum, tok)
        return tok

    def emit(self, eng, fn, reads=(), writes=(), dma=False, cc=False, extra_waits=()):
        waits = set()
        for b in reads:
            t = self.last_w.get(b)
            if t is not None:
                waits.add(t)
        for b in writes:
            t = self.last_w.get(b)
            if t is not None:
                waits.add(t)
            for t2 in self.readers.get(b, ()):
                waits.add(t2)
        for t in extra_waits:
            if t is not None:
                waits.add(t)
        stream = dma if isinstance(dma, str) else None
        fixed = set()
        for (ws, wv) in waits:
            if (ws.endswith("_dma") or ws == "cc") and self.QSRC.get(ws) != eng:
                fixed.add(self._echo(ws))
            else:
                fixed.add((ws, wv if not (ws.endswith("_dma") or ws == "cc")
                           else self.count[ws]))
        waits = fixed
        sem = "cc" if cc else (stream if stream else (eng + "_dma" if dma else eng))
        amt = 1 if cc else (16 if dma else 1)
        self.count[sem] += amt
        tok = (sem, self.count[sem])
        self.ops.append((eng, fn, tuple(waits), sem, amt))
        self.echo.pop(sem, None)
        for b in writes:
            self.last_w[b] = tok
            self.readers[b] = []
        for b in reads:
            self.readers.setdefault(b, []).append(tok)
        return tok

    def barrier(self):
        toks = []
        for e in self.ENGS:
            own = [(e, self.count[e]), (e + "_dma", self.count[e + "_dma"])]
            if e == "sync":
                own += [(s, self.count[s]) for s in ("x2t_dma", "hqt_dma", "w_dma")]
            if e == "gpsimd":
                own += [("wqin_dma", self.count["wqin_dma"])]
            toks.append(self.emit(e, lambda en: en.nop(), extra_waits=own))
        for e in self.ENGS:
            self.emit(e, lambda en: en.nop(), extra_waits=toks)

    def replay_new(self, block, sems):
        lo, hi = self.mark, len(self.ops)
        self.mark = hi
        sched = self

        def mk(eng_name):
            def go(eng):
                watermark = {}
                for idx in range(lo, hi):
                    e, fn, waits, sem, amt = sched.ops[idx]
                    if e != eng_name:
                        continue
                    for (ws, wv) in sorted(waits):
                        if watermark.get(ws, 0) >= wv:
                            continue
                        eng.wait_ge(sems[ws], wv)
                        watermark[ws] = wv
                    fn(eng).then_inc(sems[sem], amt)
            return go

        block.sync(mk("sync"))
        block.gpsimd(mk("gpsimd"))
        block.vector(mk("vector"))
        block.scalar(mk("scalar"))
        block.tensor(mk("tensor"))


def build_nc():
    nc = bass.Bass("TRN2", target_bir_lowering=False)

    dx = nc.dram_tensor("x", [T, D], F32, kind="ExternalInput").ap()
    dwq = nc.dram_tensor("wqt", [D, D], F32, kind="ExternalInput").ap()
    dwk = nc.dram_tensor("wkt", [D, D], F32, kind="ExternalInput").ap()
    dwv = nc.dram_tensor("wvt", [D, D], F32, kind="ExternalInput").ap()
    dwo = nc.dram_tensor("wot", [D, D], F32, kind="ExternalInput").ap()
    dbo = nc.dram_tensor("bo", [1, D], F32, kind="ExternalInput").ap()
    dpj = nc.dram_tensor("projp", [8, 128, M], F32, kind="ExternalInput").ap()
    dwg = nc.dram_tensor("wg_s", [IG, D], F32, kind="ExternalInput").ap()
    dwu = nc.dram_tensor("wu_s", [IG, D], F32, kind="ExternalInput").ap()
    dwd = nc.dram_tensor("wd_s", [DG, I], F32, kind="ExternalInput").ap()
    dout = nc.dram_tensor("out", [T, D], F32, kind="ExternalOutput").ap()

    NW_SH = IG * D * 2 + DG * I
    cc_w_in = nc.dram_tensor("ccw_in", [1, NW_SH], BF16)
    cc_w_out = nc.dram_tensor("ccw_out", [8, NW_SH], BF16)
    cc_kv_in = nc.dram_tensor("cckv_in", [H, 65, M], F32)
    cc_kv_out = nc.dram_tensor("cckv_out", [H, 65, M], F32)
    d_x1 = nc.dram_tensor("x1scr", [T, D], F32)

    dbg = {}
    def dump(name, ap_fn, shape, buf, dt=F32):
        if not DEBUG_DUMPS:
            return
        dbg[name] = nc.dram_tensor("dbg_" + name, shape, dt, kind="ExternalOutput")
        E("sync", lambda g, nm=name: g.dma_start(out=dbg[nm].ap(), in_=ap_fn()),
          reads=[buf], dma=True)

    es0 = ExitStack()
    sems = {}
    for s in Sched.ENGS + ("cc",):
        sems[s] = es0.enter_context(nc.semaphore("sem_" + s))
    for s in ("sync", "gpsimd", "scalar", "vector", "tensor"):
        sems[s + "_dma"] = es0.enter_context(nc.semaphore("sem_" + s + "d"))
        sems[s + "_echo"] = es0.enter_context(nc.semaphore("sem_" + s + "e"))
    for s in ("x2t_dma", "hqt_dma", "w_dma", "wqin_dma"):
        sems[s] = es0.enter_context(nc.semaphore("sem_" + s))
    S = Sched()
    E = S.emit

    def sb(stk, name, shape, dt=F32, side=None):
        return stk.enter_context(nc.sbuf_tensor(name, shape, dt, side=side))

    def psm(stk, name, shape, dt=F32):
        return stk.enter_context(nc.psum_tensor(name, shape, dt))

    pb = [psm(es0, "pb%d" % i, [128, 512], F32) for i in range(6)]
    pbf = psm(es0, "pbf", [128, 1024], BF16)
    pb7 = psm(es0, "pb7", [128, 512], F32)

    consts = sb(es0, "consts", [128, 4], F32)
    onescol = sb(es0, "onescol", [128, 1], BF16)
    ones_row = sb(es0, "ones_row", [1, 256], BF16)
    id128 = sb(es0, "id128", [128, 128], BF16)
    id128f = sb(es0, "id128f", [128, 128], F32)
    bo_row = sb(es0, "bo_row", [1, D], BF16)
    projp = sb(es0, "projp_s", [128, 8, M], BF16)
    onesblk = sb(es0, "onesblk", [128, 2], BF16)
    xs_ = [sb(es0, "xs%d" % i, [128, D], F32) for i in range(2)]
    sm = sb(es0, "sm", [128, 24], F32)

    def mkquant(stk, side=None, tag="", nsets=2):
        q = {"sets": [], "i": [0], "tag": tag, "pending": None}
        for si in range(nsets):
            s = {}
            for nm, shape, dt_ in (("m", [128, QCH // 16], F32),
                                   ("r", [128, QCH // 16], F32),
                                   ("s2", [128, QCH // 16], F32),
                                   ("ab", [128, QCH], F32),
                                   ("an", [128, QCH], F32),
                                   ("a", [128, QCH], F32),
                                   ("sg", [128, QCH], F32)):
                key = "q_%s%s%d" % (nm, tag, si)
                s[nm] = (sb(stk, key, shape, dt_, side=side), key)
            s["b"] = s["ab"]
            q["sets"].append(s)
        return q

    def emit_quant_stageB(q, hdl):
        s, Fc, P, xout, out_buf, post_fns = hdl
        NB = Fc // 16
        s2, s2N = s["s2"]; an, anN = s["an"]; b, bN = s["b"]; a, aN = s["a"]
        sg, sgN = s["sg"]
        for thr in (3.5, 2.5):
            E("vector", lambda v, th=thr: v.scalar_tensor_tensor(
                b[0:P, 0:Fc], an[0:P, 0:Fc], th, b[0:P, 0:Fc], OP.is_gt, OP.add),
              reads=[anN], writes=[bN])
        E("vector", lambda v: v.scalar_tensor_tensor(
            a[0:P, 0:Fc], b[0:P, 0:Fc], 2.0, a[0:P, 0:Fc], OP.mult, OP.add),
          reads=[bN], writes=[aN])
        E("vector", lambda v: v.tensor_tensor(
            a[0:P, 0:Fc].rearrange("p (b k) -> p b k", k=16),
            a[0:P, 0:Fc].rearrange("p (b k) -> p b k", k=16),
            s2[0:P, 0:NB].unsqueeze(2).broadcast_to([P, NB, 16]), OP.mult),
          reads=[s2N], writes=[aN])
        tok = E("vector", lambda v: v.tensor_tensor(
            xout, a[0:P, 0:Fc], sg[0:P, 0:Fc], OP.mult),
            reads=[aN, sgN], writes=[out_buf])
        for fn in post_fns:
            fn()
        return tok

    def flush_quant(q):
        hdl = q.get("pending")
        if hdl is not None:
            q["pending"] = None
            return emit_quant_stageB(q, hdl)
        return None

    def emit_quant_chunk(q, xin, xout, Fc, in_bufs, out_buf, P=128,
                         pipe=False, post_fns=()):
        NB = Fc // 16
        s = q["sets"][q["i"][0] % len(q["sets"])]
        q["i"][0] += 1
        m, mN = s["m"]; r, rN = s["r"]; s2, s2N = s["s2"]
        ab, abN = s["ab"]; an, anN = s["an"]; b, bN = s["b"]; a, aN = s["a"]
        sg, sgN = s["sg"]
        E("scalar", lambda ss: ss.activation(ab[0:P, 0:Fc], xin, AF.Abs, scale=6.0),
          reads=list(in_bufs), writes=[abN])
        E("scalar", lambda ss: ss.activation(sg[0:P, 0:Fc], xin, AF.Sign),
          reads=list(in_bufs), writes=[sgN])
        E("vector", lambda v: v.tensor_reduce(
            m[0:P, 0:NB], xin.rearrange("p (b k) -> p b k", k=16), AX.X, OP.max,
            apply_absolute_value=True), reads=list(in_bufs), writes=[mN])
        E("vector", lambda v: v.tensor_single_scalar(m[0:P, 0:NB], m[0:P, 0:NB], 6e-12, OP.max),
          writes=[mN])
        E("vector", lambda v: v.reciprocal(r[0:P, 0:NB], m[0:P, 0:NB]),
          reads=[mN], writes=[rN])
        E("vector", lambda v: v.tensor_single_scalar(s2[0:P, 0:NB], m[0:P, 0:NB], 1.0 / 12.0, OP.mult),
          reads=[mN], writes=[s2N])
        E("vector", lambda v: v.tensor_tensor(
            an[0:P, 0:Fc].rearrange("p (b k) -> p b k", k=16),
            ab[0:P, 0:Fc].rearrange("p (b k) -> p b k", k=16),
            r[0:P, 0:NB].unsqueeze(2).broadcast_to([P, NB, 16]), OP.mult),
          reads=[abN, rN], writes=[anN])
        E("scalar", lambda ss: ss.activation(
            b[0:P, 0:Fc], an[0:P, 0:Fc], AF.Sign, bias=consts[0:P, 2:3]),
          reads=[anN, "consts"], writes=[bN])
        E("vector", lambda v: v.tensor_scalar(
            a[0:P, 0:Fc], an[0:P, 0:Fc], 0.25, 2.0, OP.is_gt, OP.add),
          reads=[anN], writes=[aN])
        for thr in (0.75, 1.25, 1.75):
            E("vector", lambda v, th=thr: v.scalar_tensor_tensor(
                a[0:P, 0:Fc], an[0:P, 0:Fc], th, a[0:P, 0:Fc], OP.is_gt, OP.add),
              reads=[anN], writes=[aN])
        hdl = (s, Fc, P, xout, out_buf, tuple(post_fns))
        if pipe:
            prev = q.get("pending")
            q["pending"] = hdl
            if prev is not None:
                return emit_quant_stageB(q, prev)
            return None
        return emit_quant_stageB(q, hdl)

    def emit_quant(q, xin_fn, xout_fn, Fcols, in_bufs, out_buf, pipe=False):
        c0 = 0
        while c0 < Fcols:
            fc = min(QCH, Fcols - c0)
            emit_quant_chunk(q, xin_fn(c0, fc), xout_fn(c0, fc), fc, in_bufs,
                             out_buf, pipe=pipe)
            c0 += fc

    def emit_ln(sqbuf, xbuf, xname, outap, outbuf):
        E("vector", lambda v: v.tensor_reduce(sm[:, 0:1], xbuf[:, :], AX.X, OP.add),
          reads=[xname], writes=["sm"])
        E("vector", lambda v: v.tensor_single_scalar(sm[:, 0:1], sm[:, 0:1], 1.0 / D, OP.mult),
          writes=["sm"])
        E("scalar", lambda s: s.activation(sqbuf[:, 0:D], xbuf[:, :], AF.Square,
                                           accum_out=sm[:, 4:5]),
          reads=[xname], writes=["sq_scr", "sm"])
        E("vector", lambda v: v.tensor_single_scalar(sm[:, 4:5], sm[:, 4:5], 1.0 / D, OP.mult),
          writes=["sm"])
        E("vector", lambda v: v.scalar_tensor_tensor(
            sm[:, 5:6], sm[:, 0:1], sm[:, 0:1], sm[:, 4:5], OP.mult, OP.subtract),
          writes=["sm"])
        E("scalar", lambda s: s.activation(sm[:, 6:7], sm[:, 5:6], AF.Sqrt,
                                           bias=consts[:, 0:1], scale=-1.0),
          reads=["consts"], writes=["sm"])
        E("vector", lambda v: v.reciprocal(sm[:, 7:8], sm[:, 6:7]), writes=["sm"])
        E("vector", lambda v: v.tensor_scalar(
            sm[:, 8:9], sm[:, 0:1], sm[:, 7:8], -1.0, OP.mult, OP.mult), writes=["sm"])
        return E("scalar", lambda s: s.activation(
            outap, xbuf[:, :], AF.Identity, bias=sm[:, 8:9], scale=sm[:, 7:8]),
            reads=[xname, "sm"], writes=[outbuf])

    block_cm = nc.Block()
    block = block_cm.__enter__()

    es1 = ExitStack()
    iota_s = sb(es1, "iota_s", [128, 128], F32, side="right")
    X1T = sb(es1, "X1T", [128, 8, T], BF16, side="right")
    w_aL = sb(es1, "w_aL", [128, 8, 512], BF16, side="right")
    w_aH = sb(es1, "w_aH", [128, 8, 512], BF16, side="right")
    w_kh = sb(es1, "w_kh", [128, 8, 512], BF16, side="right")
    xn_s = sb(es1, "xn_s", [128, 2, D], BF16, side="right")
    wq_in_ = [sb(es1, "wq_in%d" % i, [128, QCH], F32, side="right") for i in range(2)]
    wq_out_ = [sb(es1, "wq_out%d" % i, [128, QCH], BF16, side="right") for i in range(2)]
    wtr_ = [sb(es1, "wtr%d" % i, [128, QCH], BF16, side="right") for i in range(2)]
    sq1 = sb(es1, "sq1", [128, 2048], BF16, side="right")
    q1 = mkquant(es1, side="right", nsets=2)
    kfw = [sb(es1, "kfw%d" % i, [128, 2, T], BF16, side="right") for i in range(2)]
    kvst = sb(es1, "kvst", [65, 4, M], F32, side="right")

    es2 = ExitStack()
    Qf = sb(es2, "Qf", [128, 8, T], BF16)
    Vt = sb(es2, "Vt", [128, NT, H, 65], BF16)
    NMXQ = sb(es2, "NMXQ", [128, NT, H], F32)
    QSQ = sb(es2, "QSQ", [128, NT, H], F32)
    NMXK = sb(es2, "NMXK", [128, NT, H], F32)
    KSQ = sb(es2, "KSQ", [128, NT, H], F32)
    rawk = sb(es2, "rawk", [128, 4, M], BF16)
    vhat = sb(es2, "vhat", [128, 4, 65], BF16)
    vsum_r = sb(es2, "vsum_r", [1, H * 65], BF16)
    u_raw = sb(es2, "u_raw", [128, NT, H], F32)
    u_bf = sb(es2, "u_bf", [128, NT, H], BF16)
    urows = sb(es2, "urows", [16, NT, 128], BF16)

    E("vector", lambda v: v.memset(consts[:, 0:1], 1e-5), writes=["consts"])
    E("vector", lambda v: v.memset(consts[:, 1:2], -LN16), writes=["consts"])
    E("vector", lambda v: v.memset(consts[:, 2:3], -5.0), writes=["consts"])
    E("vector", lambda v: v.memset(onescol[:, :], 1.0), writes=["onescol"])
    E("vector", lambda v: v.memset(ones_row[:, :], 1.0), writes=["ones_row"])
    E("vector", lambda v: v.memset(onesblk[:, :], 0.0), writes=["onesblk"])
    E("vector", lambda v: v.memset(onesblk[0:64, 0:1], 1.0), writes=["onesblk"])
    E("vector", lambda v: v.memset(onesblk[64:128, 1:2], 1.0), writes=["onesblk"])
    E("gpsimd", lambda g: g.iota(iota_s[:, :], [[1, 128]], channel_multiplier=-1,
                                 allow_small_or_imprecise_dtypes=True), writes=["iota"])
    E("vector", lambda v: v.tensor_single_scalar(id128[:, :], iota_s[:, :], 0.0, OP.is_equal),
      reads=["iota"], writes=["id128"])
    E("vector", lambda v: v.tensor_single_scalar(id128f[:, :], iota_s[:, :], 0.0, OP.is_equal),
      reads=["iota"], writes=["id128f"])
    E("gpsimd", lambda g: g.dma_start(out=bo_row[:, :], in_=dbo), writes=["bo_row"], dma=True)
    E("gpsimd", lambda g: g.dma_start(out=projp[:, :, :], in_=dpj.rearrange("a p m -> p a m")),
      writes=["projp"], dma=True)
    E("gpsimd", lambda g: g.dma_start(
        out=w_aL[:, :, :], in_=dwq[:, 0:512].rearrange("(c p) e -> p c e", p=128)),
      writes=["w_aL"], dma=True)
    E("gpsimd", lambda g: g.dma_start(
        out=w_aH[:, :, :], in_=dwq[:, 512:1024].rearrange("(c p) e -> p c e", p=128)),
      writes=["w_aH"], dma=True)

    wq_tasks = []
    wbufi = [0]
    woff = 0
    secoff = 0
    for (dsrc, nrows, ncols) in ((dwg, IG, D), (dwu, IG, D), (dwd, DG, I)):
        secoff = {id(dwg): 0, id(dwu): IG * D, id(dwd): 2 * IG * D}[id(dsrc)]
        r0 = 0
        while r0 < nrows:
            rr = min(128, nrows - r0)
            c0 = 0
            while c0 < ncols:
                fc = min(QCH, ncols - c0)

                def wq_task(ds=dsrc, a=r0, b=rr, cc=c0, f=fc, sec=secoff,
                            nct=ncols // 128, il=nrows):
                    i = wbufi[0] % 2
                    wbufi[0] += 1
                    win, winN = wq_in_[i], "wq_in%d" % i
                    wout, woutN = wq_out_[i], "wq_out%d" % i
                    wtr, wtrN = wtr_[i], "wtr%d" % i
                    E("gpsimd", lambda g: g.dma_start(
                        out=win[0:b, 0:f], in_=ds[a:a + b, cc:cc + f]),
                      writes=[winN], dma="wqin_dma")
                    ncl = f // 128

                    def post():
                        for cl in range(ncl):
                            E("tensor", lambda te, cl_=cl: te.transpose(
                                pbf[0:128, cl_ * 128:cl_ * 128 + b],
                                wout[0:b, cl_ * 128:(cl_ + 1) * 128], id128[0:b, 0:b]),
                              reads=[woutN, "id128"], writes=["pbf"])
                        E("vector", lambda v: v.tensor_copy(
                            wtr[:, 0:ncl * b].rearrange("p (c i) -> p c i", c=ncl),
                            pbf[0:128, 0:ncl * 128].rearrange(
                                "p (c i) -> p c i", c=ncl)[:, :, 0:b]),
                          reads=["pbf"], writes=[wtrN])
                        dst = cc_w_in.ap()[0:1, sec:sec + 128 * nct * il].rearrange(
                            "o (p c i) -> (o p) c i", p=128, c=nct)[
                                :, cc // 128:cc // 128 + ncl, a:a + b]
                        E("gpsimd", lambda g, d_=dst: g.dma_start(
                            out=d_, in_=wtr[:, 0:ncl * b].rearrange(
                                "p (c i) -> p c i", c=ncl)),
                          reads=[wtrN], writes=["ccw"], dma=True)
                    emit_quant_chunk(q1, win[0:b, 0:f], wout[0:b, 0:f], f,
                                     [winN], woutN, P=b, pipe=True,
                                     post_fns=(post,))
                wq_tasks.append(wq_task)
                woff += rr * fc
                c0 += fc
            r0 += rr

    for t in range(NT):
        xbuf = xs_[t % 2]
        xname = "xs%d" % (t % 2)
        E("sync", lambda g, xb=xbuf, tt=t: g.dma_start(
            out=xb[:, :], in_=dx[tt * 128:(tt + 1) * 128, :]), writes=[xname], dma=True)
        xnb = xn_s[:, t % 2, :]
        xnn = "xn%d" % (t % 2)
        emit_ln(sq1, xbuf, xname, xnb, xnn)
        E("scalar", lambda s, ob=xnb, tt=t: s.dma_start_transpose(
            X1T[:, :, tt * 128:(tt + 1) * 128], ob),
          reads=[xnn], writes=["X1T"], dma=True)
        for _ in range(2):
            if wq_tasks:
                wq_tasks.pop(0)()
    while wq_tasks:
        wq_tasks.pop(0)()
    flush_quant(q1)

    for e in range(8):
        whalf, wname = (w_aL, "w_aL") if e < 4 else (w_aH, "w_aH")
        for g in range(4):
            bank = pb[(e * 4 + g) % 4]
            bname = "pb%d" % ((e * 4 + g) % 4)
            for c in range(8):
                E("tensor", lambda te, ee=e, gg=g, cc=c, wb=whalf, bk=bank: te.matmul(
                    bk[:, :], wb[:, cc, (ee % 4) * 128:(ee % 4 + 1) * 128],
                    X1T[:, cc, gg * 512:(gg + 1) * 512],
                    start=(cc == 0), stop=(cc == 7)),
                  reads=[wname, "X1T"] if c in (0, 7) else [],
                  writes=[bname] if c in (0, 7) else [])
            E("scalar", lambda s, ee=e, gg=g, bk=bank: s.activation(
                Qf[:, ee, gg * 512:(gg + 1) * 512], bk[:, :], AF.Identity, scale=0.125),
              reads=[bname], writes=["Qf"])

    E("gpsimd", lambda g: g.dma_start(
        out=w_aL[:, :, :], in_=dwv[:, 0:512].rearrange("(c p) e -> p c e", p=128)),
      writes=["w_aL"], dma=True)
    E("gpsimd", lambda g: g.dma_start(
        out=w_aH[:, :, :], in_=dwv[:, 512:1024].rearrange("(c p) e -> p c e", p=128)),
      writes=["w_aH"], dma=True)
    for t in range(NT):
        for g in range(2):
            whalf, wname = (w_aL, "w_aL") if g == 0 else (w_aH, "w_aH")
            bank = pb[(t * 2 + g) % 4]
            bname = "pb%d" % ((t * 2 + g) % 4)
            for c in range(8):
                E("tensor", lambda te, tt=t, cc=c, bk=bank, wb=whalf: te.matmul(
                    bk[:, :], X1T[:, cc, tt * 128:(tt + 1) * 128],
                    wb[:, cc, :],
                    start=(cc == 0), stop=(cc == 7)),
                  reads=[wname, "X1T"] if c in (0, 7) else [],
                  writes=[bname] if c in (0, 7) else [])
            E("vector", lambda v, tt=t, gg=g, bk=bank: v.tensor_copy(
                Vt[:, tt, gg * 8:(gg + 1) * 8, 0:64],
                bk[:, :].rearrange("p (h e) -> p h e", e=64)),
              reads=[bname], writes=["Vt"])
        E("vector", lambda v, tt=t: v.memset(Vt[:, tt, :, 64:65], 1.0), writes=["Vt"])

    dump("Qf", lambda: Qf[:, :, :], [128, 8, T], "Qf", BF16)
    dump("Vt", lambda: Vt[:, :, :, :], [128, NT, H, 65], "Vt", BF16)
    dump("X1T", lambda: X1T[:, :, :], [128, 8, T], "X1T", BF16)

    vw = ((pb[0], "pb0", 0, 512), (pb[1], "pb1", 512, 512), (pb[2], "pb2", 1024, 16))
    for (bank, bname, off, w) in vw:
        for t in range(NT):
            E("tensor", lambda te, tt=t, bk=bank, o=off, ww=w: te.matmul(
                bk[0:1, 0:ww], onescol[:, :],
                Vt[:, tt, :, :].rearrange("p h e -> p (h e)")[:, o:o + ww],
                start=(tt == 0), stop=(tt == NT - 1)),
              reads=["Vt", "onescol"] if t in (0, NT - 1) else [],
              writes=[bname] if t in (0, NT - 1) else [])
    E("vector", lambda v: v.tensor_single_scalar(vsum_r[0:1, 0:512], pb[0][0:1, :], E16, OP.mult),
      reads=["pb0"], writes=["vsum_r"])
    E("vector", lambda v: v.tensor_single_scalar(vsum_r[0:1, 512:1024], pb[1][0:1, :], E16, OP.mult),
      reads=["pb1"], writes=["vsum_r"])
    E("vector", lambda v: v.tensor_single_scalar(vsum_r[0:1, 1024:1040], pb[2][0:1, 0:16], E16, OP.mult),
      reads=["pb2"], writes=["vsum_r"])

    E("gpsimd", lambda g: g.dma_start(
        out=w_kh[:, :, :], in_=dwk[:, 0:512].rearrange("(c p) e -> p c e", p=128)),
      writes=["w_kh"], dma=True)
    for wave in range(4):
        heads = [wave * 4 + i for i in range(4)]
        kbuf = kfw[wave % 2]
        kname = "kfw%d" % (wave % 2)
        if wave == 2:
            E("gpsimd", lambda g: g.dma_start(
                out=w_kh[:, :, :], in_=dwk[:, 512:1024].rearrange("(c p) e -> p c e", p=128)),
              writes=["w_kh"], dma=True)
        for ei, e in enumerate((2 * wave, 2 * wave + 1)):
            for g in range(4):
                bank = pb[4 + (g % 2)]
                bname = "pb%d" % (4 + (g % 2))
                for c in range(8):
                    E("tensor", lambda te, ee=e, gg=g, cc=c, bk=bank: te.matmul(
                        bk[:, :], w_kh[:, cc, (ee % 4) * 128:(ee % 4 + 1) * 128],
                        X1T[:, cc, gg * 512:(gg + 1) * 512],
                        start=(cc == 0), stop=(cc == 7)),
                      reads=["w_kh", "X1T"] if c in (0, 7) else [],
                      writes=[bname] if c in (0, 7) else [])
                E("scalar", lambda s, ei_=ei, gg=g, bk=bank, kb_=kbuf: s.activation(
                    kb_[:, ei_, gg * 512:(gg + 1) * 512], bk[:, :], AF.Identity, scale=0.125),
                  reads=[bname], writes=[kname])
        for ei in range(2):
            E("scalar", lambda s, kb_=kbuf, ei_=ei: s.activation(
                sq1[:, :], kb_[:, ei_, :], AF.Square),
              reads=[kname], writes=["sq_scr"])
            cglob = 2 * wave + ei
            for t in range(NT):
                E("tensor", lambda te, tt=t, cg=cglob: te.matmul(
                    pb7[:, tt * 16 + 2 * cg: tt * 16 + 2 * cg + 2],
                    sq1[:, tt * 128:(tt + 1) * 128], onesblk[:, :],
                    start=True, stop=True),
                  reads=["sq_scr", "onesblk"] if t in (0, NT - 1) else [],
                  writes=["pb7"] if ((t == 0 and ei == 0) or (t == NT - 1 and ei == 1)) else [])
        E("vector", lambda v, w_=wave: v.tensor_copy(
            KSQ[:, :, 4 * w_:4 * w_ + 4],
            pb7[:, 0:256].rearrange("p (t h) -> p t h", h=16)[:, :, 4 * w_:4 * w_ + 4]),
          reads=["pb7"], writes=["KSQ"])
        XWB = (pb[4], pb[5], pb7)
        XWN = ("pb4", "pb5", "pb7")

        def kw_xw(k, t, sub):
            hh2 = heads[2 * sub:2 * sub + 2]
            for j, h in enumerate(hh2):
                i = 2 * sub + j
                bank = XWB[(2 * k + j) % 3]
                bname = XWN[(2 * k + j) % 3]
                hr = (h % 2) * 64
                ei = i // 2
                hp = 2 * wave + i // 2
                E("tensor", lambda te, bk=bank, kb_=kbuf, ei_=ei, hr_=hr, hp_=hp, tt=t: te.matmul(
                    bk[:, 0:256],
                    kb_[hr_:hr_ + 64, ei_, tt * 128:(tt + 1) * 128],
                    projp[hr_:hr_ + 64, hp_, :],
                    start=True, stop=True, tile_position=(hr_, 0)),
                  reads=[kname, "projp"], writes=[bname])

        def kw_tail(k, t, sub):
            hh2 = heads[2 * sub:2 * sub + 2]
            for j, h in enumerate(hh2):
                bname = XWN[(2 * k + j) % 3]
                E("vector", lambda v, bk=XWB[(2 * k + j) % 3], h_=h, tt=t: v.tensor_reduce(
                    NMXK[:, tt, h_:h_ + 1],
                    bk[:, 0:256], AX.X, OP.max, negate=True),
                  reads=[bname], writes=["NMXK"])
            E("vector", lambda v, h0=hh2[0], tt=t, sb=sub: v.scalar_tensor_tensor(
                sm[:, 12 + 2 * sb:14 + 2 * sb], KSQ[:, tt, h0:h0 + 2], -0.5,
                NMXK[:, tt, h0:h0 + 2], OP.mult, OP.add),
              reads=["KSQ", "NMXK"], writes=["smk%d" % sub])
            E("scalar", lambda s, sb=sub: s.activation(
                sm[:, 16 + 2 * sb:18 + 2 * sb], sm[:, 12 + 2 * sb:14 + 2 * sb],
                AF.Exp, bias=consts[:, 1:2]),
              reads=["smk%d" % sub, "consts"], writes=["smk2%d" % sub])
            for j, h in enumerate(hh2):
                i = 2 * sub + j
                bname = XWN[(2 * k + j) % 3]
                E("scalar", lambda s, bk=XWB[(2 * k + j) % 3], i_=i: s.activation(
                    rawk[:, i_, :], bk[:, 0:256], AF.Exp),
                  reads=[bname], writes=["rawk%d" % i])

        def kw_tail2(t, sub):
            hh2 = heads[2 * sub:2 * sub + 2]
            for j, h in enumerate(hh2):
                i = 2 * sub + j
                E("vector", lambda v, i_=i, h_=h, tt=t, sb=sub: v.tensor_scalar(
                    vhat[:, i_, :], Vt[:, tt, h_, :], sm[:, 16 + i_:17 + i_], 1.0,
                    OP.mult, OP.mult),
                  reads=["Vt", "smk2%d" % sub], writes=["vhat%d" % i])
            for j, h in enumerate(hh2):
                i = 2 * sub + j
                bank = pb[i]
                bname = "pb%d" % i
                E("tensor", lambda te, i_=i, bk=bank, tt=t: te.matmul(
                    bk[0:65, 0:256], vhat[:, i_, :], rawk[:, i_, :],
                    start=(tt == 0), stop=False),
                  reads=["vhat%d" % i, "rawk%d" % i],
                  writes=[bname] if t == 0 else [])

        items = [(t, sub) for t in range(NT) for sub in range(2)]
        kw_xw(0, *items[0])
        prev = None
        for k, it in enumerate(items):
            kw_tail(k, *it)
            if k + 1 < len(items):
                kw_xw(k + 1, *items[k + 1])
            if prev is not None:
                kw_tail2(*prev)
            prev = it
        kw_tail2(*prev)
        for i, h in enumerate(heads):
            bank = pb[i]
            bname = "pb%d" % i
            E("tensor", lambda te, bk=bank, h_=h: te.matmul(
                bk[0:65, 0:256], vsum_r[0:1, h_ * 65:(h_ + 1) * 65],
                ones_row[0:1, :], start=False, stop=True),
              reads=["vsum_r", "ones_row"], writes=[bname])
            E("vector", lambda v, bk=bank, i_=i: v.tensor_copy(
                kvst[:, i_, :], bk[0:65, 0:256]),
              reads=[bname], writes=["kvst"])
        E("gpsimd", lambda g, w_=wave: g.dma_start(
            out=cc_kv_in.ap()[4 * w_:4 * w_ + 4, :, :].rearrange("h d m -> d h m"),
            in_=kvst[:, :, :]),
          reads=["kvst"], writes=["cckv"], dma=True)

    dump("KSQ", lambda: KSQ[:, :, :], [128, NT, H], "KSQ")
    dump("NMXK", lambda: NMXK[:, :, :], [128, NT, H], "NMXK")
    dump("kvin", lambda: cc_kv_in.ap(), [H, 65, M], "cckv")
    E("gpsimd", lambda g: g.collective_compute(
        "AllReduce", OP.add, replica_groups=[[0, 1, 2, 3], [4, 5, 6, 7]],
        ins=[cc_kv_in.ap().opt()], outs=[cc_kv_out.ap().opt()]),
      reads=["cckv"], writes=["cckv_out"], cc=True)

    for c in range(8):
        E("scalar", lambda s, cc=c: s.activation(sq1[:, :], Qf[:, cc, :], AF.Square),
          reads=["Qf"], writes=["sq_scr"])
        for t in range(NT):
            E("tensor", lambda te, tt=t, cc=c: te.matmul(
                pb7[:, tt * 16 + 2 * cc: tt * 16 + 2 * cc + 2],
                sq1[:, tt * 128:(tt + 1) * 128], onesblk[:, :],
                start=True, stop=True),
              reads=["sq_scr", "onesblk"] if t in (0, NT - 1) else [],
              writes=["pb7"] if ((t == 0 and c == 0) or (t == NT - 1 and c == 7)) else [])
    E("vector", lambda v: v.tensor_copy(
        QSQ[:, :, :].rearrange("p t h -> p (t h)"), pb7[:, 0:256]),
      reads=["pb7"], writes=["QSQ"])

    for wave in range(4):
        heads = [wave * 4 + i for i in range(4)]
        XWB = (0, 1, 4, 5)
        for t in range(NT):
            for i, h in enumerate(heads):
                bank = pb[XWB[i]]
                bname = "pb%d" % XWB[i]
                hp, hr = h // 2, (h % 2) * 64
                E("tensor", lambda te, bk=bank, hp_=hp, hr_=hr, tt=t: te.matmul(
                    bk[:, 0:256],
                    Qf[hr_:hr_ + 64, hp_, tt * 128:(tt + 1) * 128],
                    projp[hr_:hr_ + 64, hp_, :],
                    start=True, stop=True, tile_position=(hr_, 0)),
                  reads=["Qf", "projp"], writes=[bname])
            for i, h in enumerate(heads):
                bname = "pb%d" % XWB[i]
                E("vector", lambda v, bk=pb[XWB[i]], h_=h, tt=t: v.tensor_reduce(
                    NMXQ[:, tt, h_:h_ + 1],
                    bk[:, 0:256], AX.X, OP.max, negate=True),
                  reads=[bname], writes=["NMXQ"])

    E("vector", lambda v: v.scalar_tensor_tensor(
        u_raw[:, :, :].rearrange("p t h -> p (t h)"),
        QSQ[:, :, :].rearrange("p t h -> p (t h)"), 0.5,
        NMXQ[:, :, :].rearrange("p t h -> p (t h)"), OP.mult, OP.subtract),
      reads=["QSQ", "NMXQ"], writes=["u_raw"])
    E("scalar", lambda s: s.activation(
        u_bf[:, :, :].rearrange("p t h -> p (t h)"),
        u_raw[:, :, :].rearrange("p t h -> p (t h)"), AF.Exp),
      reads=["u_raw"], writes=["u_bf"])
    E("scalar", lambda s: s.activation(
        u_raw[:, :, :].rearrange("p t h -> p (t h)"),
        u_raw[:, :, :].rearrange("p t h -> p (t h)"), AF.Exp),
      reads=["u_raw"], writes=["u_raw"])
    for t in range(NT):
        E("tensor", lambda te, tt=t: te.transpose(
            pbf[0:16, 0:128], u_bf[:, tt, :], id128[:, :]),
          reads=["u_bf", "id128"], writes=["pbf"])
        E("vector", lambda v, tt=t: v.tensor_copy(urows[:, tt, :], pbf[0:16, 0:128]),
          reads=["pbf"], writes=["urows"])

    dump("NMXQ", lambda: NMXQ[:, :, :], [128, NT, H], "NMXQ")
    dump("u_raw", lambda: u_raw[:, :, :], [128, NT, H], "u_raw")
    dump("u_bf", lambda: u_bf[:, :, :], [128, NT, H], "u_bf", BF16)
    dump("urows", lambda: urows[:, :, :], [16, NT, 128], "urows", BF16)
    S.barrier()
    S.replay_new(block, sems)
    es1.close()

    es3 = ExitStack()
    w_b = sb(es3, "w_b", [128, 8, D], BF16, side="right")
    kvb = sb(es3, "kvb", [65, H, M], BF16, side="right")
    Ccol = sb(es3, "Ccol", [65, H], F32, side="right")
    ccol_bf = sb(es3, "ccol_bf", [65, H], BF16, side="right")
    ct_sb = sb(es3, "ct_sb", [16, 65], BF16, side="right")
    cflat = sb(es3, "cflat", [1, 16 * 65], BF16, side="right")
    kvt = sb(es3, "kvt", [128, 2, H, 65], BF16, side="right")
    dstg = [sb(es3, "dstg0", [1, 512], F32, side="right")]
    rawq = sb(es3, "rawq", [128, 2, 2, T], BF16, side="right")
    Mg = sb(es3, "Mg", [128, 8, T], BF16, side="right")
    denF = sb(es3, "denF", [16, T], F32, side="right")
    denT = sb(es3, "denT", [128, NT, H], F32, side="right")
    r_bf = sb(es3, "r_bf", [128, NT, H], BF16, side="right")
    rrows = sb(es3, "rrows", [16, NT, 128], BF16, side="right")
    uflat2 = sb(es3, "uflat2", [1, 2 * T], BF16, side="right")
    rflat2 = None
    srep = sb(es3, "srep", [128, H], F32, side="right")
    rbc = sb(es3, "rbc", [128, 512], F32, side="right")

    E("gpsimd", lambda g: g.dma_start(
        out=w_b[:, :, :], in_=dwo.rearrange("(c p) e -> p c e", p=128)),
      writes=["w_b"], dma=True)
    E("gpsimd", lambda g: g.dma_start(
        out=kvb[:, :, :], in_=cc_kv_out.ap().rearrange("h d m -> d h m")),
      reads=["cckv_out"], writes=["kvb"], dma=True)
    E("vector", lambda v: v.nop(), reads=["kvb", "w_b"])
    E("gpsimd", lambda g: g.collective_compute(
        "AllGather", OP.bypass, replica_groups=[list(range(8))],
        ins=[cc_w_in.ap().opt()], outs=[cc_w_out.ap().opt()]),
      reads=["ccw"], writes=["ccw_out"], cc=True)
    E("vector", lambda v: v.tensor_reduce(Ccol[:, :], kvb[:, :, :], AX.X, OP.add),
      reads=["kvb"], writes=["Ccol"])
    E("vector", lambda v: v.tensor_single_scalar(Ccol[64:65, :], Ccol[64:65, :], 16.0, OP.add),
      writes=["Ccol"])
    E("vector", lambda v: v.tensor_single_scalar(Ccol[:, :], Ccol[:, :], 16.0 * E16, OP.mult),
      writes=["Ccol"])
    E("vector", lambda v: v.tensor_copy(ccol_bf[:, :], Ccol[:, :]),
      reads=["Ccol"], writes=["ccol_bf"])
    E("tensor", lambda te: te.transpose(pbf[0:16, 0:65], ccol_bf[:, :], id128[0:65, 0:65]),
      reads=["ccol_bf", "id128"], writes=["pbf"])
    E("vector", lambda v: v.tensor_copy(ct_sb[:, :], pbf[0:16, 0:65]),
      reads=["pbf"], writes=["ct_sb"])
    E("scalar", lambda s: s.dma_start(
        out=cflat[0:1, :].rearrange("o (h e) -> o h e", h=16), in_=ct_sb[:, :]),
      reads=["ct_sb", "trq"], writes=["cflat", "trq"], dma=True)
    for h in range(H):
        for c in range(2):
            E("tensor", lambda te, h_=h, c_=c: te.transpose(
                pbf[0:128, 0:65], kvb[:, h_, c_ * 128:(c_ + 1) * 128], id128[0:65, 0:65]),
              reads=["kvb", "id128"], writes=["pbf"])
            E("vector", lambda v, h_=h, c_=c: v.tensor_copy(kvt[:, c_, h_, :], pbf[0:128, 0:65]),
              reads=["pbf"], writes=["kvt"])
    E("tensor", lambda te: te.matmul(
        pb7[:, 0:16], ones_row[0:1, 0:128],
        cflat[0:1, :].rearrange("o (h e) -> o h e", e=65)[:, :, 64:65]
            .rearrange("o h z -> o (h z)"),
        start=True, stop=True),
      reads=["cflat", "ones_row"], writes=["pb7"])
    E("vector", lambda v: v.tensor_copy(srep[:, :], pb7[:, 0:16]),
      reads=["pb7"], writes=["srep"])

    for p in range(8):
        hA, hB = 2 * p, 2 * p + 1
        E("scalar", lambda s, pp=p: s.dma_start(
            out=uflat2[0:1, :].rearrange("o (i t k) -> o i t k", i=2, t=NT),
            in_=urows[2 * pp:2 * pp + 2, :, :]),
          reads=["urows", "trq"], writes=["uflat2", "trq"], dma=True)
        for k, (c, i, g) in enumerate(
                [(c, i, g) for c in range(2) for i in range(2) for g in range(4)]):
            hr = (0, 64)[i]
            bank = (pb[0], pb[1], pb7)[k % 3]
            bname = ("pb0", "pb1", "pb7")[k % 3]
            E("tensor", lambda te, bk=bank, c_=c, hr_=hr, pp=p, gg=g: te.matmul(
                bk[:, 0:512],
                projp[hr_:hr_ + 64, pp, c_ * 128:(c_ + 1) * 128],
                Qf[hr_:hr_ + 64, pp, gg * 512:(gg + 1) * 512],
                start=True, stop=True, tile_position=(hr_, 0)),
              reads=["projp", "Qf"], writes=[bname])
            E("scalar", lambda s, bk=bank, c_=c, i_=i, gg=g: s.activation(
                rawq[:, c_, i_, gg * 512:(gg + 1) * 512], bk[:, 0:512], AF.Exp),
              reads=[bname], writes=["rawq"])
        for g in range(4):
            bank = pb[2 + g]
            bname = "pb%d" % (2 + g)
            for i in range(2):
                h_ = (hA, hB)[i]
                E("tensor", lambda te, bk=bank, i_=i, h__=h_, gg=g: te.matmul(
                    bk[0:65, :],
                    kvt[:, 0, h__, 0:65],
                    rawq[:, 0, i_, gg * 512:(gg + 1) * 512],
                    start=True, stop=False),
                  reads=["kvt", "rawq"], writes=[bname])
                E("tensor", lambda te, bk=bank, i_=i, h__=h_, gg=g: te.matmul(
                    bk[0:64, :],
                    cflat[0:1, h__ * 65:h__ * 65 + 64],
                    uflat2[0:1, i_ * T + gg * 512:i_ * T + (gg + 1) * 512],
                    start=False, stop=False),
                  reads=["cflat", "uflat2"])
                E("tensor", lambda te, bk=bank, i_=i, h__=h_, gg=g: te.matmul(
                    bk[0:65, :],
                    kvt[:, 1, h__, 0:65],
                    rawq[:, 1, i_, gg * 512:(gg + 1) * 512],
                    start=False, stop=True),
                  reads=["kvt", "rawq"], writes=[bname])
                E("vector", lambda v, bk=bank, pp=p, i_=i, gg=g: v.tensor_copy(
                    Mg[64 * i_:64 * i_ + 64, pp, gg * 512:(gg + 1) * 512],
                    bk[0:64, :]),
                  reads=[bname], writes=["Mg"])
                ds = dstg[0]
                dsn = "dstg0"
                E("vector", lambda v, bk=bank, d_=ds: v.tensor_copy(
                    d_[0:1, :], bk[64:65, :]),
                  reads=[bname], writes=[dsn])
                E("sync", lambda gq, d_=ds, h__=h_, gg=g: gq.dma_start(
                    out=denF[h__:h__ + 1, gg * 512:(gg + 1) * 512], in_=d_[0:1, :]),
                  reads=[dsn], writes=["denF"], dma=True)

    dump("kvb", lambda: kvb[:, :, :], [65, H, M], "kvb", BF16)
    dump("cflat", lambda: cflat[0:1, :], [1, 16 * 65], "cflat", BF16)
    dump("kvt", lambda: kvt[:, :, :, :], [128, 2, H, 65], "kvt", BF16)
    dump("MgN", lambda: Mg[:, :, :], [128, 8, T], "Mg", BF16)

    for t in range(NT):
        E("tensor", lambda te, tt=t: te.transpose(
            pb7[0:128, 0:16], denF[:, tt * 128:(tt + 1) * 128], id128f[0:16, 0:16]),
          reads=["denF", "id128f"], writes=["pb7"])
        E("vector", lambda v, tt=t: v.tensor_copy(denT[:, tt, :], pb7[0:128, 0:16]),
          reads=["pb7"], writes=["denT"])
    for t in range(NT):
        E("vector", lambda v, tt=t: v.tensor_tensor(
            sm[:, 0:16], u_raw[:, tt, :], srep[:, :], OP.mult),
          reads=["u_raw", "srep"], writes=["sm"])
        E("vector", lambda v, tt=t: v.tensor_tensor(
            denT[:, tt, :], denT[:, tt, :], sm[:, 0:16], OP.add),
          reads=["sm"], writes=["denT"])
    E("vector", lambda v: v.reciprocal(
        denT[:, :, :].rearrange("p t h -> p (t h)"),
        denT[:, :, :].rearrange("p t h -> p (t h)")), writes=["denT"])
    E("vector", lambda v: v.tensor_copy(
        r_bf[:, :, :].rearrange("p t h -> p (t h)"),
        denT[:, :, :].rearrange("p t h -> p (t h)")),
      reads=["denT"], writes=["r_bf"])
    for t in range(NT):
        E("tensor", lambda te, tt=t: te.transpose(
            pbf[0:16, 0:128], r_bf[:, tt, :], id128[:, :]),
          reads=["r_bf", "id128"], writes=["pbf"])
        E("vector", lambda v, tt=t: v.tensor_copy(rrows[:, tt, :], pbf[0:16, 0:128]),
          reads=["pbf"], writes=["rrows"])

    dump("denF", lambda: denF[:, :], [16, T], "denF", F32)
    dump("denT", lambda: denT[:, :, :], [128, NT, H], "denT")

    for p in range(8):
        hA, hB = 2 * p, 2 * p + 1
        E("scalar", lambda s, pp=p: s.dma_start(
            out=uflat2[0:1, :].rearrange("o (i t k) -> o i t k", i=2, t=NT),
            in_=rrows[2 * pp:2 * pp + 2, :, :]),
          reads=["rrows", "trq"], writes=["uflat2", "trq"], dma=True)
        for g in range(4):
            for i, pbase in enumerate((0, 64)):
                E("tensor", lambda te, pb_=pbase, i_=i, gg=g: te.matmul(
                    pb7[pb_:pb_ + 64, :],
                    ones_row[0:1, 0:64],
                    uflat2[0:1, i_ * T + gg * 512:i_ * T + (gg + 1) * 512],
                    start=True, stop=True, tile_position=(0, pb_)),
                  reads=["uflat2", "ones_row"], writes=["pb7"])
            E("scalar", lambda s: s.copy(rbc[:, :], pb7[:, :]),
              reads=["pb7"], writes=["rbc"])
            E("vector", lambda v, pp=p, gg=g: v.tensor_tensor(
                Mg[:, pp, gg * 512:(gg + 1) * 512],
                Mg[:, pp, gg * 512:(gg + 1) * 512],
                rbc[:, :], OP.mult),
              reads=["rbc"], writes=["Mg"])

    dump("Mg", lambda: Mg[:, :, :], [128, 8, T], "Mg", BF16)

    for t in range(NT):
        xbuf = xs_[t % 2]
        xname = "xs%d" % (t % 2)
        E("sync", lambda g, xb=xbuf, tt=t: g.dma_start(
            out=xb[:, :], in_=dx[tt * 128:(tt + 1) * 128, :]), writes=[xname], dma=True)
        for g in range(2):
            bank = pb[g]
            bname = "pb%d" % g
            for c in range(8):
                E("tensor", lambda te, bk=bank, cc=c, tt=t, gg=g: te.matmul(
                    bk[:, :], Mg[:, cc, tt * 128:(tt + 1) * 128],
                    w_b[:, cc, gg * 512:(gg + 1) * 512],
                    start=(cc == 0), stop=False),
                  reads=["Mg", "w_b"] if c in (0, 7) else [],
                  writes=[bname] if c == 0 else [])
            E("tensor", lambda te, bk=bank, gg=g: te.matmul(
                bk[:, :], ones_row[0:1, 0:128], bo_row[0:1, gg * 512:(gg + 1) * 512],
                start=False, stop=True),
              reads=["bo_row", "ones_row"], writes=[bname])
            E("vector", lambda v, xb=xbuf, bk=bank, gg=g: v.tensor_tensor(
                xb[:, gg * 512:(gg + 1) * 512], xb[:, gg * 512:(gg + 1) * 512],
                bk[:, :], OP.add),
              reads=[bname, xname], writes=[xname])
        E("sync", lambda g, xb=xbuf, tt=t: g.dma_start(
            out=d_x1.ap()[tt * 128:(tt + 1) * 128, :], in_=xb[:, :]),
          reads=[xname], writes=["x1_%d" % t], dma=True)

    dump("x1", lambda: d_x1.ap(), [T, D], "x1_%d" % (NT - 1))
    S.barrier()
    S.replay_new(block, sems)
    es2.close()
    es3.close()

    es4 = ExitStack()
    wgT = sb(es4, "wgT", [128, 8, I], BF16)
    wuT = sb(es4, "wuT", [128, 8, I], BF16)
    wdT = sb(es4, "wdT", [128, 22, D], BF16)
    x2t_ = [sb(es4, "x2t%d" % i, [128, 8, 128], BF16) for i in range(2)]
    xq2_ = [sb(es4, "xq2%d" % i, [128, D], BF16) for i in range(2)]
    xn2_ = [sb(es4, "xn2%d" % i, [128, D], F32) for i in range(2)]
    hch_ = [sb(es4, "hch%d" % i, [128, 512], F32) for i in range(2)]
    hqc_ = [sb(es4, "hqc%d" % i, [128, 512], BF16) for i in range(3)]
    hqT_ = [sb(es4, "hqT%d" % i, [128, 4, 128], BF16) for i in range(3)]
    gs_ = [sb(es4, "gs%d" % i, [128, 512], F32) for i in range(2)]
    sq3 = sb(es4, "sq3", [128, D], BF16)
    q3 = mkquant(es4, tag="3", nsets=3)

    for r in range(8):
        for (dstT, dname, nrows, ncols, sec, qeng) in (
                (wgT, "wgT", IG, D, 0, "sync"),
                (wuT, "wuT", IG, D, IG * D, "gpsimd"),
                (wdT, "wdT", DG, I, 2 * IG * D, "sync")):
            nct = ncols // 128
            src = cc_w_out.ap()[r:r + 1, sec:sec + 128 * nct * nrows].rearrange(
                "o (p c i) -> (o p) c i", p=128, c=nct)
            E(qeng, lambda s, sr=src, dT=dstT, a0=r * nrows, a1=(r + 1) * nrows:
                s.dma_start(out=dT[:, :, a0:a1], in_=sr),
              reads=["ccw_out"], writes=[dname],
              dma=("w_dma" if qeng == "sync" else True))
    dump("wgTd", lambda: wgT[:, :, :], [128, 8, I], "wgT", BF16)
    dump("wuTd", lambda: wuT[:, :, :], [128, 8, I], "wuT", BF16)
    dump("wdTd", lambda: wdT[:, :, :], [128, 22, D], "wdT", BF16)

    NGRP = 6

    def ffn_down_chunk(t, g):
        wl = 512 if g < 5 else 256
        hqT = hqT_[g % 3]
        hqTn = "hqT%d" % (g % 3)
        ncl = wl // 128
        for cl in range(ncl):
            cc = g * 4 + cl
            for half in range(2):
                bank = pb[4 + half]
                bname = "pb%d" % (4 + half)
                first = (g == 0 and cl == 0)
                last = (g == NGRP - 1 and cl == ncl - 1)
                E("tensor", lambda te, bk=bank, hT=hqT, cl_=cl, cc_=cc, hf=half,
                    f=first, l=last: te.matmul(
                    bk[:, :], hT[:, cl_, :], wdT[:, cc_, hf * 512:(hf + 1) * 512],
                    start=f, stop=l),
                  reads=[hqTn, "wdT"],
                  writes=[bname] if (first or last) else [])

    def ffn_residual_out(t):
        xbuf = xs_[t % 2]
        xname = "xs%d" % (t % 2)
        for half in range(2):
            bank = pb[4 + half]
            bname = "pb%d" % (4 + half)
            E("vector", lambda v, xb=xbuf, bk=bank, hf=half: v.tensor_tensor(
                xb[:, hf * 512:(hf + 1) * 512], xb[:, hf * 512:(hf + 1) * 512],
                bk[:, :], OP.add),
              reads=[bname, xname], writes=[xname])
        E("gpsimd", lambda g, xb=xbuf, tt=t: g.dma_start(
            out=dout[tt * 128:(tt + 1) * 128, :], in_=xb[:, :]),
          reads=[xname], writes=["out_%d" % t], dma=True)

    def ffn_front(t):
        xbuf = xs_[t % 2]
        xname = "xs%d" % (t % 2)
        xn2 = xn2_[t % 2]
        xn2n = "xn2%d" % (t % 2)
        xq2 = xq2_[t % 2]
        xq2n = "xq2%d" % (t % 2)
        x2t = x2t_[t % 2]
        x2tn = "x2t%d" % (t % 2)
        E("gpsimd", lambda g, xb=xbuf, tt=t: g.dma_start(
            out=xb[:, :], in_=d_x1.ap()[tt * 128:(tt + 1) * 128, :]),
          reads=["x1_%d" % t], writes=[xname], dma=True)
        emit_ln(sq3, xbuf, xname, xn2[:, :], xn2n)
        emit_quant(q3, lambda c0, fc: xn2[:, c0:c0 + fc],
                   lambda c0, fc: xq2[:, c0:c0 + fc], D, [xn2n], xq2n, pipe=True)
        flush_quant(q3)
        E("sync", lambda s, xt=x2t, xq=xq2: s.dma_start_transpose(
            xt[:, :, :], xq[:, :]),
          reads=[xq2n], writes=[x2tn], dma="x2t_dma")

    for t in range(NT):
        x2t = x2t_[t % 2]
        x2tn = "x2t%d" % (t % 2)
        ffn_front(t)
        if t >= 1:
            ffn_residual_out(t - 1)
        for g in range(NGRP):
            w0 = g * 512
            wlen = 512 if g < 5 else 256
            bkg = pb[(2 * g) % 4]
            bng = "pb%d" % ((2 * g) % 4)
            bku = pb[(2 * g + 1) % 4]
            bnu = "pb%d" % ((2 * g + 1) % 4)
            for c in range(8):
                E("tensor", lambda te, bk=bkg, xt=x2t, cc=c, w0_=w0, wl=wlen: te.matmul(
                    bk[:, 0:wl], xt[:, cc, :], wgT[:, cc, w0_:w0_ + wl],
                    start=(cc == 0), stop=(cc == 7)),
                  reads=[x2tn, "wgT"] if c in (0, 7) else [],
                  writes=[bng] if c in (0, 7) else [])
            for c in range(8):
                E("tensor", lambda te, bk=bku, xt=x2t, cc=c, w0_=w0, wl=wlen: te.matmul(
                    bk[:, 0:wl], xt[:, cc, :], wuT[:, cc, w0_:w0_ + wl],
                    start=(cc == 0), stop=(cc == 7)),
                  reads=[x2tn, "wuT"] if c in (0, 7) else [],
                  writes=[bnu] if c in (0, 7) else [])
            if g >= 2:
                ffn_down_chunk(t, g - 2)
            gsb, gsn = gs_[g % 2], "gs%d" % (g % 2)
            hch, hchn = hch_[g % 2], "hch%d" % (g % 2)
            hqc, hqcn = hqc_[g % 3], "hqc%d" % (g % 3)
            hqT, hqTn = hqT_[g % 3], "hqT%d" % (g % 3)
            E("scalar", lambda s, bk=bkg, gb=gsb, wl=wlen: s.activation(
                gb[:, 0:wl], bk[:, 0:wl], AF.Silu),
              reads=[bng], writes=[gsn])
            E("vector", lambda v, bk=bku, hb=hch, gb=gsb, wl=wlen: v.tensor_tensor(
                hb[:, 0:wl], bk[:, 0:wl], gb[:, 0:wl], OP.mult),
              reads=[bnu, gsn], writes=[hchn])
            emit_quant_chunk(
                q3, hch[0:128, 0:wlen], hqc[0:128, 0:wlen], wlen,
                [hchn], hqcn, pipe=True,
                post_fns=(lambda hT=hqT, hb=hqc, wl=wlen, hqcn_=hqcn,
                          hqTn_=hqTn: E(
                              "sync", lambda s: s.dma_start_transpose(
                                  hT[:, 0:wl // 128, :], hb[0:128, 0:wl]),
                              reads=[hqcn_], writes=[hqTn_], dma="hqt_dma"),))
        ffn_down_chunk(t, NGRP - 2)
        flush_quant(q3)
        ffn_down_chunk(t, NGRP - 1)

    ffn_residual_out(NT - 1)
    dump("xn20d", lambda: xn2_[0][:, :], [128, D], "xn20")
    dump("xq20d", lambda: xq2_[0][:, :], [128, D], "xq20", BF16)
    dump("x2t0d", lambda: x2t_[0][:, :, :], [128, 8, 128], "x2t0", BF16)
    dump("hqT0d", lambda: hqT_[0][:, :, :], [128, 4, 128], "hqT0", BF16)
    dump("hqT1d", lambda: hqT_[1][:, :, :], [128, 4, 128], "hqT1", BF16)
    E("gpsimd", lambda g: g.nop(), reads=["out_%d" % (NT - 1)],
      extra_waits=[("gpsimd_dma", S.count["gpsimd_dma"])])
    S.replay_new(block, sems)
    es4.close()

    block_cm.__exit__(None, None, None)
    es0.close()
    return nc


_NC_CACHE = {}


def kernel(L0_attn_x, L0_attn_Wq, L0_attn_Wk, L0_attn_Wv, L0_attn_Wo, L0_attn_bo,
           L0_attn_proj, L0_ffn_w_gate, L0_ffn_w_up, L0_ffn_w_down):
    x = np.ascontiguousarray(np.asarray(L0_attn_x, dtype=np.float32))
    wqt = np.ascontiguousarray(np.asarray(L0_attn_Wq, np.float32).T)
    wkt = np.ascontiguousarray(np.asarray(L0_attn_Wk, np.float32).T)
    wvt = np.ascontiguousarray(np.asarray(L0_attn_Wv, np.float32).T)
    wot = np.ascontiguousarray(np.asarray(L0_attn_Wo, np.float32).T)
    bo = np.ascontiguousarray(np.asarray(L0_attn_bo, np.float32).reshape(1, D))
    pj = np.asarray(L0_attn_proj, np.float32).transpose(0, 2, 1)
    projp = np.ascontiguousarray(pj.reshape(8, 128, M))
    wg = np.asarray(L0_ffn_w_gate, np.float32)
    wu = np.asarray(L0_ffn_w_up, np.float32)
    wd = np.asarray(L0_ffn_w_down, np.float32)

    if "nc" not in _NC_CACHE:
        _NC_CACHE["nc"] = build_nc()
    nc = _NC_CACHE["nc"]

    in_maps = []
    for c in range(NCORE):
        b, s = c // 4, c % 4
        in_maps.append({
            "x": np.ascontiguousarray(x[b, s * T:(s + 1) * T, :]),
            "wqt": wqt, "wkt": wkt, "wvt": wvt, "wot": wot,
            "bo": bo, "projp": projp,
            "wg_s": np.ascontiguousarray(wg[c * IG:(c + 1) * IG, :]),
            "wu_s": np.ascontiguousarray(wu[c * IG:(c + 1) * IG, :]),
            "wd_s": np.ascontiguousarray(wd[c * DG:(c + 1) * DG, :]),
        })
    res = run_bass_kernel_spmd(nc, in_maps, core_ids=list(range(NCORE)))
    out = np.empty((B, TFULL, D), np.float32)
    for c in range(NCORE):
        b, s = c // 4, c % 4
        out[b, s * T:(s + 1) * T, :] = res.results[c]["out"]
    return out
